# version 05 (final)
# speedup vs baseline: 1.1993x; 1.1478x over previous
import numpy as np
import ml_dtypes
import concourse.bass as bass
import concourse.mybir as mybir
from concourse.bass_utils import run_bass_kernel_spmd

F32 = mybir.dt.float32
BF16 = mybir.dt.bfloat16
I32 = mybir.dt.int32
AX = mybir.AluOpType
AF = mybir.ActivationFunctionType
XA = mybir.AxisListType.X

T = 2048
DM = 2048
NCHUNK = DM // 128

ENG = ("sync", "tensor", "vector", "scalar", "gpsimd")
GQA_NO_SWAP = False
GQA_NO_VPROJ = False
GQA_SIMPLE_ROPE = False
CAST_ENGINE = "gpsimd"


class Sem:
    def __init__(self, nc, name):
        self.h = nc.alloc_semaphore(name)
        self.n = 0

    def inc(self, k=1):
        self.n += k
        return self.n


class DmaSlots:

    def __init__(self, nc, name, nslots):
        self.sems = [Sem(nc, f"{name}_{i}") for i in range(nslots)]
        self.ns = nslots

    def fill(self, slot, ndma):
        s = self.sems[slot % self.ns]
        tgt = s.inc(16 * ndma)
        return s.h, tgt


class Prog:

    def __init__(self, nc):
        self.nc = nc
        self.ops = {e: [] for e in ENG}
        self.WD = Sem(nc, "WD")
        self.XD = Sem(nc, "XD")
        self.GD = Sem(nc, "GD")
        self.SD = Sem(nc, "SD")
        self.PE = Sem(nc, "PE")
        self.VE = Sem(nc, "VE")
        self.SE = Sem(nc, "SE")
        self.GE = Sem(nc, "GE")
        self.bar = Sem(nc, "bar")
        self.ws = DmaSlots(nc, "ws", 2)
        self.ws2 = DmaSlots(nc, "ws2", 2)
        self.xs = DmaSlots(nc, "xs", 2)
        self.ss = DmaSlots(nc, "ss", 2)
        self.gs = DmaSlots(nc, "gs", 2)

    def add(self, eng, fn):
        self.ops[eng].append(fn)

    def on(self, *engs):
        def dec(fn):
            for e in engs:
                self.ops[e].append(fn)
        return dec

    def barrier(self):
        tgt = self.bar.n + len(ENG)
        self.bar.n = tgt
        s = self.bar.h

        def f_pe(e, t=tgt, s=s):
            e.nop().then_inc(s, 1)
            e.wait_ge(s, t)

        def f_drain(e, t=tgt, s=s):
            e.drain().then_inc(s, 1)
            e.wait_ge(s, t)

        def f_sync(e, t=tgt, s=s):
            e.nop().then_inc(s, 1)
            e.wait_ge(s, t)

        self.add("tensor", f_pe)
        self.add("vector", f_drain)
        self.add("scalar", f_drain)
        self.add("gpsimd", f_drain)
        self.add("sync", f_sync)

    def finish(self):
        nc = self.nc
        with nc.Block() as blk:
            @blk.sync
            def _(e):
                for f in self.ops["sync"]:
                    f(e)

            @blk.tensor
            def _(e):
                for f in self.ops["tensor"]:
                    f(e)

            @blk.vector
            def _(e):
                for f in self.ops["vector"]:
                    f(e)

            @blk.scalar
            def _(e):
                for f in self.ops["scalar"]:
                    f(e)

            @blk.gpsimd
            def _(e):
                for f in self.ops["gpsimd"]:
                    f(e)
        return nc


class Arena:

    def __init__(self, nc, name, nbytes):
        self.nb = nbytes
        self.t = nc.alloc_sbuf_tensor(name, [128, nbytes // 2], BF16)
        self.off = 0
        self.marks = []

    def push(self):
        self.marks.append(self.off)

    def pop(self):
        self.off = self.marks.pop()

    def take(self, rows, cols, dt, align=64):
        esz = {F32: 4, BF16: 2, I32: 4}[dt]
        self.off = (self.off + align - 1) // align * align
        start = self.off
        self.off += cols * esz
        assert self.off <= self.nb, f"arena overflow {self.off} > {self.nb}"
        ap = self.t.ap() if dt == BF16 else self.t.ap().bitcast(dt)
        c0 = start // esz
        return ap[0:rows, c0:c0 + cols]

    def at(self, byte_off, rows, cols, dt):
        esz = {F32: 4, BF16: 2, I32: 4}[dt]
        ap = self.t.ap() if dt == BF16 else self.t.ap().bitcast(dt)
        c0 = byte_off // esz
        return ap[0:rows, c0:c0 + cols]


class Consts:

    def __init__(self, nc, P: Prog, host):
        self.nc = nc
        self.aps = {}
        self.sb = {}
        loads = []
        for name, arr in host.items():
            dt = {np.dtype(np.float32): F32, np.dtype(ml_dtypes.bfloat16): BF16,
                  np.dtype(np.int32): I32}[arr.dtype]
            d = nc.dram_tensor("cst_" + name, list(arr.shape), dt, kind="ExternalInput").ap()
            t = nc.alloc_sbuf_tensor("cst_sb_" + name, list(arr.shape), dt)
            self.aps[name] = d
            self.sb[name] = t.ap()
            loads.append((d, t))
        n = len(loads)
        xd = P.XD
        base = xd.n
        xd.inc(16 * n)

        @P.on("sync")
        def _(e):
            for d, t in loads:
                e.dma_start(t.ap(), d).then_inc(xd.h, 16)
            e.wait_ge(xd.h, base + 16 * n)
        P.barrier()

    def __getitem__(self, k):
        return self.sb[k]


def _swa_jset(tqc):
    if tqc == 0:
        return list(range(16))
    lo, hi = 4 * tqc - 2, 4 * tqc + 5
    s = set(range(max(lo, 0), min(hi, 15) + 1))
    s.add(0)
    return sorted(s)


def _swa_ij_included(i, j):
    return abs(i - j) <= 2 or i == 0 or j == 0


def _swa_mask_np(i, j):
    s = np.arange(128)[:, None] + 128 * j
    t = np.arange(128)[None, :] + 128 * i
    m = (np.abs(t - s) < 256) | (t < 16) | (s < 16)
    return m.astype(np.float32)


def host_swa_masks():
    pats = {}
    idx = {}
    for i in range(16):
        for j in range(16):
            if not _swa_ij_included(i, j):
                continue
            m = _swa_mask_np(i, j)
            if m.all():
                idx[(i, j)] = None
                continue
            key = m.tobytes()
            if key not in pats:
                pats[key] = (len(pats), m)
            idx[(i, j)] = pats[key][0]
    arrs = [None] * len(pats)
    for _, (k, m) in pats.items():
        arrs[k] = m.astype(ml_dtypes.bfloat16)
    return np.concatenate(arrs, axis=1), idx


SWA_MASKS, SWA_MIDX = None, None


def _host_sel8():
    s = np.zeros((8, 8 * 128), np.float32)
    for h in range(8):
        s[h, h * 128:(h + 1) * 128] = 1.0
    return s


def host_consts():
    ident = np.eye(128, dtype=ml_dtypes.bfloat16)
    ones_col = np.ones((128, 1), dtype=ml_dtypes.bfloat16)
    ones_row_f32 = np.ones((1, 128), dtype=np.float32)
    iu = np.triu(np.ones((128, 128), dtype=np.float32)).astype(ml_dtypes.bfloat16)
    cvals = np.zeros((1, 16), dtype=np.float32)
    cvals[0, 0] = 1e-5
    cvals[0, 1] = 1e-6
    cvals[0, 2] = 1.0
    inv = 1.0 / (10000.0 ** (np.arange(0, 64, 2, dtype=np.float64) / 64.0))
    tt = np.arange(T, dtype=np.float64)
    fr = np.outer(inv, tt)
    cosr = np.cos(fr)
    sinr = np.sin(fr)
    cos128 = np.tile(cosr, (4, 1))
    sin128 = np.concatenate([-sinr, sinr, -sinr, sinr], axis=0)
    swap = np.zeros((128, 128), dtype=np.float32)
    for i in range(128):
        swap[i, i ^ 32] = 1.0
    qs = 1.0 / np.sqrt(64.0)
    global SWA_MASKS, SWA_MIDX
    if SWA_MASKS is None:
        SWA_MASKS, SWA_MIDX = host_swa_masks()
    return {
        "swa_masks": SWA_MASKS,
        "qscale_pc": np.full((128, 1), 1.0 / np.sqrt(128.0), np.float32),
        "ngate_pc": np.full((128, 1), -6.906768, np.float32),
        "eps_pc": np.full((128, 1), 1e-5, np.float32),
        "ones_pc": np.full((128, 1), 1.0, np.float32),
        "half_pc": np.full((128, 1), 0.5, np.float32),
        "nhalf_pc": np.full((128, 1), -0.5, np.float32),
        "sel8": _host_sel8(),
        "ident": ident,
        "ident_f32": np.eye(128, dtype=np.float32),
        "ones_col": ones_col,
        "ones_row_f32": ones_row_f32,
        "ones_sq_f32": np.ones((128, 128), dtype=np.float32),
        "ones_sq": np.ones((128, 128), dtype=ml_dtypes.bfloat16),
        "mask_ult": iu,
        "cvals": cvals,
        "swap32": swap.astype(ml_dtypes.bfloat16),
    }


def host_gqa_tables():
    inv = 1.0 / (10000.0 ** (np.arange(0, 64, 2, dtype=np.float64) / 64.0))
    tt = np.arange(T, dtype=np.float64)
    fr = np.outer(inv, tt)
    cosr = np.cos(fr)
    sinr = np.sin(fr)
    cos128 = np.tile(cosr, (4, 1))
    sin128 = np.concatenate([-sinr, sinr, -sinr, sinr], axis=0)
    qs = 1.0 / np.sqrt(64.0)
    return {
        "gq_cos_q": (cos128 * qs).astype(np.float32),
        "gq_sin_q": (sin128 * qs).astype(np.float32),
        "gq_cos_k": cos128.astype(np.float32),
        "gq_sin_k": sin128.astype(np.float32),
    }


def emit_ln(P: Prog, ar: Arena, cst, ps, xres_ap, t0, tw, XF, eps=1e-5, xf_fn=None):
    if xf_fn is None:
        xf_fn = lambda c, o, XF=XF, tw=tw: XF[:, c * tw + o:c * tw + o + 512]
    nc = P.nc
    nq = tw // 512
    assert nq <= 2, "ln supports tw<=1024"
    ar.push()
    XLD = [ar.take(128, tw, F32) for _ in range(2)]
    XBH = ar.take(128, NCHUNK * tw, BF16)
    XSQ = ar.take(128, 2 * tw, BF16)
    SR = ar.take(1, 8 * tw, F32)
    ROW_M = SR[0:1, 0 * tw:1 * tw]
    ROW_V = SR[0:1, 1 * tw:2 * tw]
    ROW_R = SR[0:1, 2 * tw:3 * tw]
    ROW_M2 = SR[0:1, 3 * tw:4 * tw]
    ROW_T = SR[0:1, 4 * tw:5 * tw]

    xrt = xres_ap.rearrange("(c p) t -> c p t", p=128)
    xd, se, pe, ve = P.XD, P.SE, P.PE, P.VE
    ld_tgt = [P.xs.fill(c, 1) for c in range(NCHUNK)]

    se_base = se.n
    pe_base = pe.n
    se.inc(2 * NCHUNK)

    @P.on("sync")
    def _(e):
        for c in range(NCHUNK):
            b = c % 2
            if c >= 2:
                e.wait_ge(se.h, se_base + 2 * (c - 1))
            sh, _t = ld_tgt[c]
            e.dma_start(XLD[b], xrt[c, :, t0:t0 + tw]).then_inc(sh, 16)

    @P.on("scalar")
    def _(e):
        for c in range(NCHUNK):
            b = c % 2
            e.wait_ge(*ld_tgt[c])
            xb = XLD[b]
            e.activation(XBH[:, c * tw:(c + 1) * tw], xb, AF.Copy).then_inc(se.h, 1)
            if c >= 2:
                e.wait_ge(pe.h, pe_base + (c - 1))
            e.activation(XSQ[:, (c % 2) * tw:(c % 2 + 1) * tw], xb, AF.Square).then_inc(se.h, 1)

    @P.on("tensor")
    def _(e):
        ones = cst["ones_col"]
        for c in range(NCHUNK):
            e.wait_ge(se.h, se_base + 2 * (c + 1))
            for q in range(nq):
                e.matmul(ps[q].ap()[0:1, 0:512], ones,
                         XBH[:, c * tw + q * 512: c * tw + q * 512 + 512],
                         start=(c == 0), stop=(c == NCHUNK - 1))
            for q in range(nq):
                mm = e.matmul(ps[2 + q].ap()[0:1, 0:512], ones,
                              XSQ[:, (c % 2) * tw + q * 512:(c % 2) * tw + q * 512 + 512],
                              start=(c == 0), stop=(c == NCHUNK - 1))
            mm.then_inc(pe.h, 1)
    pe_stats = pe.inc(NCHUNK)

    ve0 = ve.n
    ve.inc(4)

    @P.on("vector")
    def _(e):
        e.wait_ge(pe.h, pe_stats)
        for q in range(nq):
            e.tensor_scalar(ROW_M[0:1, q * 512:(q + 1) * 512], ps[q].ap()[0:1, 0:512],
                            1.0 / DM, None, AX.mult)
        for q in range(nq):
            i_ = e.tensor_scalar(ROW_T[0:1, q * 512:(q + 1) * 512], ps[2 + q].ap()[0:1, 0:512],
                                 1.0 / DM, None, AX.mult)
        i_.then_inc(ve.h, 1)
        e.wait_ge(ve.h, ve0 + 1)
        e.tensor_tensor(ROW_V, ROW_M, ROW_M, AX.mult).then_inc(ve.h, 1)
        e.wait_ge(ve.h, ve0 + 2)
        e.tensor_sub(ROW_V, ROW_T, ROW_V).then_inc(ve.h, 1)
    ve_var = ve0 + 3

    @P.on("scalar")
    def _(e):
        e.wait_ge(ve.h, ve_var)
        e.activation(ROW_T, ROW_V, AF.Sqrt, bias=cst["cvals"][0:1, 0:1]).then_inc(se.h, 1)
    se_sqrt = se.inc(1)

    @P.on("vector")
    def _(e):
        e.wait_ge(se.h, se_sqrt)
        e.reciprocal(ROW_R, ROW_T).then_inc(ve.h, 1)
        e.wait_ge(ve.h, ve0 + 4)
        e.tensor_mul(ROW_M2, ROW_M, ROW_R).then_inc(ve.h, 1)
    ve_rows = ve.inc(1)

    @P.on("tensor")
    def _(e):
        onesf = cst["ones_row_f32"]
        e.wait_ge(ve.h, ve_rows)
        for q in range(nq):
            e.matmul(ps[4 + q].ap(), onesf, ROW_R[0:1, q * 512:(q + 1) * 512],
                     start=True, stop=True)
        for q in range(nq):
            mm = e.matmul(ps[6 + q].ap(), onesf, ROW_M2[0:1, q * 512:(q + 1) * 512],
                          start=True, stop=True)
        mm.then_inc(pe.h, 1)
    pe_bcast = pe.inc(1)

    TMP = XSQ.bitcast(F32)
    ve_n0 = ve.n
    ve.inc(2 * NCHUNK * nq)

    @P.on("vector")
    def _(e):
        e.wait_ge(pe.h, pe_bcast)
        k = ve_n0
        for c in range(NCHUNK):
            for q in range(nq):
                xb = XBH[:, c * tw + q * 512: c * tw + q * 512 + 512]
                tmp = TMP[:, q * 512:(q + 1) * 512]
                if k > ve_n0:
                    e.wait_ge(ve.h, k)
                e.tensor_tensor(tmp, xb, ps[4 + q].ap(), AX.mult).then_inc(ve.h, 1)
                k += 1
                e.wait_ge(ve.h, k)
                e.tensor_tensor(xf_fn(c, q * 512),
                                tmp, ps[6 + q].ap(), AX.subtract).then_inc(ve.h, 1)
                k += 1
    ve_done = ve.n
    ar.pop()
    return ve_done


def emit_dequant(P: Prog, ar: Arena, cst, ps, wqT_ap, s_ap, out_ap, K, E):
    nc = P.nc
    P.barrier()
    ar.push()
    SB_S = ar.take(1, E, F32)
    SBC = ar.take(128, E, F32)
    KC = K // 128
    JB = 512 if KC <= 16 else 256
    nj = E // JB
    WIN = [ar.take(128, KC * JB, I32) for _ in range(2)]
    WOUT = [ar.take(128, KC * JB, BF16) for _ in range(2)]

    xd, pe, ve, wd = P.XD, P.PE, P.VE, P.WD

    xd_s = xd.inc(16)

    @P.on("sync")
    def _(e):
        e.dma_start(SB_S, s_ap.unsqueeze(0)).then_inc(xd.h, 16)

    pe_base = pe.n

    @P.on("tensor")
    def _(e):
        onesf = cst["ones_row_f32"]
        e.wait_ge(xd.h, xd_s)
        for j in range(E // 512):
            bank = j % 4
            if j >= 4:
                e.wait_ge(ve.h, ve_sbc_base + (j - 4) + 1)
            e.matmul(ps[bank].ap(), onesf, SB_S[0:1, j * 512:(j + 1) * 512],
                     start=True, stop=True).then_inc(pe.h, 1)
    pe.inc(E // 512)

    ve_sbc_base = P.VE.n

    @P.on("vector")
    def _(e):
        for j in range(E // 512):
            e.wait_ge(pe.h, pe_base + j + 1)
            e.tensor_copy(SBC[:, j * 512:(j + 1) * 512], ps[j % 4].ap()).then_inc(ve.h, 1)
    ve_sbc = P.VE.inc(E // 512)

    wqr = wqT_ap.rearrange("(c p) e -> c p e", p=128)
    outr = out_ap.rearrange("(c p) e -> c p e", p=128)

    ve_dq_base = P.VE.n
    load_tgt = [P.ws.fill(j, KC) for j in range(nj)]
    store_tgt = [P.ss.fill(j, KC) for j in range(nj)]

    @P.on("sync")
    def _(e):
        for j in range(nj):
            b = j % 2
            if j >= 2:
                e.wait_ge(P.VE.h, ve_dq_base + (j - 2) + 1)
            sh, _tg = load_tgt[j]
            for c in range(KC):
                e.dma_start(WIN[b][:, c * JB:(c + 1) * JB],
                            wqr[c, :, j * JB:(j + 1) * JB]).then_inc(sh, 16)

    @P.on("vector")
    def _(e):
        e.wait_ge(ve.h, ve_sbc)
        for j in range(nj):
            b = j % 2
            e.wait_ge(*load_tgt[j])
            if j >= 2:
                e.wait_ge(*store_tgt[j - 2])
            e.scalar_tensor_tensor(
                WOUT[b].rearrange("p (c e) -> p c e", c=KC),
                WIN[b].rearrange("p (c e) -> p c e", c=KC),
                -128.0,
                SBC[:, j * JB:(j + 1) * JB].unsqueeze(1).broadcast_to([128, KC, JB]),
                AX.add, AX.mult).then_inc(ve.h, 1)
    ve_dq = P.VE.inc(nj)

    @P.on("scalar")
    def _(e):
        for j in range(nj):
            b = j % 2
            e.wait_ge(P.VE.h, ve_dq_base + j + 1)
            sh, _tg = store_tgt[j]
            for c in range(KC):
                e.dma_start(outr[c, :, j * JB:(j + 1) * JB],
                            WOUT[b][:, c * JB:(c + 1) * JB]).then_inc(sh, 16)
        e.wait_ge(*store_tgt[nj - 1])
        if nj >= 2:
            e.wait_ge(*store_tgt[nj - 2])

    ar.pop()


HID = 5632
NEB = HID // 512
NEC = HID // 128


def emit_ffn(P: Prog, ar: Arena, cst, ps, xin_ap, xout_ap, wg_ap, wu_ap, wd_ap):
    for th in range(2):
        _emit_ffn_half(P, ar, cst, ps, xin_ap, xout_ap, wg_ap, wu_ap, wd_ap, th)
    P.barrier()


def _emit_ffn_half(P: Prog, ar: Arena, cst, ps, xin_ap, xout_ap, wg_ap, wu_ap, wd_ap, th):
    nc = P.nc
    if True:
        t0 = th * 1024
        P.barrier()
        ar.push()
        XF = ar.take(128, NCHUNK * 1024, BF16)
        emit_ln(P, ar, cst, ps, xin_ap, t0, 1024, XF)
        P.barrier()

        H = ar.take(128, NEC * 1024, BF16)
        SIL = ar.take(128, 2 * 512, BF16)
        WB = [ar.take(128, NCHUNK * 256 * 2, BF16) for _ in range(2)]

        wgpr = wg_ap.rearrange("(c p) e -> p c e", p=128)
        wupr = wu_ap.rearrange("(c p) e -> p c e", p=128)

        wd_, pe, ve, se = P.WD, P.PE, P.VE, P.SE
        nhb = HID // 256
        pe_base = pe.n
        ve_base = ve.n
        se_base = se.n
        NG = nhb * 4
        wb_tgt = [P.ws.fill(hb, 2) for hb in range(nhb)]

        @P.on("sync")
        def _(e):
            for hb in range(nhb):
                b = hb % 2
                if hb >= 2:
                    e.wait_ge(pe.h, pe_base + (hb - 2 + 1) * 4)
                sh, _tg = wb_tgt[hb]
                wbv = WB[b].rearrange("p (c w) -> p c w", w=512)
                e.dma_start(wbv[:, :, 0:256],
                            wgpr[:, :, hb * 256:(hb + 1) * 256]).then_inc(sh, 16)
                e.dma_start(wbv[:, :, 256:512],
                            wupr[:, :, hb * 256:(hb + 1) * 256]).then_inc(sh, 16)

        @P.on("tensor")
        def _(e):
            for hb in range(nhb):
                b = hb % 2
                e.wait_ge(*wb_tgt[hb])
                for esub in range(2):
                    for tq in range(2):
                        g = hb * 4 + esub * 2 + tq
                        bank = g % 4
                        if g >= 4:
                            e.wait_ge(se.h, se_base + (g - 4) + 1)
                            e.wait_ge(ve.h, ve_base + (g - 4) + 1)
                        for c in range(NCHUNK):
                            e.matmul(ps[bank].ap(),
                                     WB[b][:, c * 512 + esub * 128:c * 512 + esub * 128 + 128],
                                     XF[:, c * 1024 + tq * 512:c * 1024 + tq * 512 + 512],
                                     start=(c == 0), stop=(c == NCHUNK - 1))
                            mm = e.matmul(ps[4 + bank].ap(),
                                          WB[b][:, c * 512 + 256 + esub * 128:c * 512 + 256 + esub * 128 + 128],
                                          XF[:, c * 1024 + tq * 512:c * 1024 + tq * 512 + 512],
                                          start=(c == 0), stop=(c == NCHUNK - 1))
                        mm.then_inc(pe.h, 1)
        pe.inc(NG)

        @P.on("scalar")
        def _(e):
            for g in range(NG):
                bank = g % 4
                e.wait_ge(pe.h, pe_base + g + 1)
                if g >= 2:
                    e.wait_ge(ve.h, ve_base + (g - 2) + 1)
                e.activation(SIL[:, (g % 2) * 512:(g % 2 + 1) * 512],
                             ps[bank].ap(), AF.Silu).then_inc(se.h, 1)
        se.inc(NG)

        @P.on("vector")
        def _(e):
            for g in range(NG):
                hb, r = divmod(g, 4)
                esub, tq = divmod(r, 2)
                ec = hb * 2 + esub
                bank = g % 4
                e.wait_ge(se.h, se_base + g + 1)
                e.tensor_tensor(H[:, ec * 1024 + tq * 512:ec * 1024 + tq * 512 + 512],
                                SIL[:, (g % 2) * 512:(g % 2 + 1) * 512],
                                ps[4 + bank].ap(), AX.mult).then_inc(ve.h, 1)
        ve.inc(NG)

        P.barrier()

        WDB = [ar.take(128, NEC * 128, BF16) for _ in range(2)]
        XT = [ar.take(128, 1024, F32) for _ in range(2)]

        wdpr = wd_ap.rearrange("(c p) o -> p c o", p=128)
        xint = xin_ap.rearrange("(c p) t -> c p t", p=128)
        xoutt = xout_ap.rearrange("(c p) t -> c p t", p=128)

        pe_base2 = pe.n
        ve_base2 = ve.n
        xd = P.XD
        wdb_tgt = [P.ws2.fill(oc, 1) for oc in range(NCHUNK)]
        xt_tgt = [P.xs.fill(oc, 1) for oc in range(NCHUNK)]
        st_base = xd.n
        xd.inc(16 * NCHUNK)

        @P.on("sync")
        def _(e):
            for oc in range(NCHUNK):
                b = oc % 2
                if oc >= 2:
                    e.wait_ge(ve.h, ve_base2 + (oc - 2) * 2 + 2)
                    e.dma_start(xoutt[oc - 2, :, t0:t0 + 1024], XT[b]).then_inc(xd.h, 16)
                    e.wait_ge(pe.h, pe_base2 + (oc - 1) * 2)
                sh, _tg = wdb_tgt[oc]
                e.dma_start(WDB[b].rearrange("p (c w) -> p c w", w=128),
                            wdpr[:, :, oc * 128:(oc + 1) * 128]).then_inc(sh, 16)
                if oc >= 2:
                    e.wait_ge(xd.h, st_base + 16 * (oc - 1))
                xh, _xt = xt_tgt[oc]
                e.dma_start(XT[b], xint[oc, :, t0:t0 + 1024]).then_inc(xh, 16)
            for oc in (NCHUNK - 2, NCHUNK - 1):
                e.wait_ge(ve.h, ve_base2 + oc * 2 + 2)
                e.dma_start(xoutt[oc, :, t0:t0 + 1024], XT[oc % 2]).then_inc(xd.h, 16)
            e.wait_ge(xd.h, st_base + 16 * NCHUNK)

        @P.on("tensor")
        def _(e):
            for oc in range(NCHUNK):
                b = oc % 2
                e.wait_ge(*wdb_tgt[oc])
                for tq in range(2):
                    bank = (oc % 4) * 2 + tq
                    if oc >= 4:
                        e.wait_ge(ve.h, ve_base2 + (oc - 4) * 2 + tq + 1)
                    for c in range(NEC):
                        mm = e.matmul(ps[bank].ap(),
                                      WDB[b][:, c * 128:(c + 1) * 128],
                                      H[:, c * 1024 + tq * 512:c * 1024 + tq * 512 + 512],
                                      start=(c == 0), stop=(c == NEC - 1))
                    mm.then_inc(pe.h, 1)
        pe.inc(NCHUNK * 2)

        @P.on("vector")
        def _(e):
            for oc in range(NCHUNK):
                b = oc % 2
                for tq in range(2):
                    bank = (oc % 4) * 2 + tq
                    e.wait_ge(pe.h, pe_base2 + oc * 2 + tq + 1)
                    e.wait_ge(*xt_tgt[oc])
                    e.tensor_tensor(XT[b][:, tq * 512:(tq + 1) * 512],
                                    XT[b][:, tq * 512:(tq + 1) * 512],
                                    ps[bank].ap(), AX.add).then_inc(ve.h, 1)
        ve.inc(NCHUNK * 2)

        ar.pop()


def emit_cast_to_bf16(P: Prog, ar: Arena, src_ap, dst_ap, R, C):
    P.barrier()
    ar.push()
    RC = R // 128
    B0 = [ar.take(128, C, BF16) for _ in range(2)]
    srcr = src_ap.rearrange("(c p) e -> c p e", p=128)
    dstr = dst_ap.rearrange("(c p) e -> c p e", p=128)
    load_t = [P.gs.fill(c, 1) for c in range(RC)]
    store_t = [P.ss.fill(c, 1) for c in range(RC)]

    @P.on("gpsimd")
    def _(e):
        for c in range(RC):
            b = c % 2
            if c >= 2:
                e.wait_ge(*store_t[c - 2])
            sh, _t = load_t[c]
            e.dma_start(B0[b], srcr[c]).then_inc(sh, 16)

    @P.on("sync")
    def _(e):
        for c in range(RC):
            b = c % 2
            e.wait_ge(*load_t[c])
            sh, _t = store_t[c]
            e.dma_start(dstr[c], B0[b]).then_inc(sh, 16)
        e.wait_ge(*store_t[RC - 1])
        if RC >= 2:
            e.wait_ge(*store_t[RC - 2])
    ar.pop()
    P.barrier()


def emit_moe(P: Prog, ar: Arena, cst, ps, xin_ap, xout_ap, wba_bf, aw72_ap,
             bflat_ap, bpc_ap):
    nc = P.nc
    P.barrier()
    ar.push()
    AW72 = ar.take(128, NCHUNK * 72, BF16)
    AW72F = ar.take(128, NCHUNK * 72, F32)
    BF = ar.take(64, DM, BF16)
    BFF = ar.take(64, DM, F32)
    BB = ar.take(128, NCHUNK, F32)
    xd, pe, ve, se = P.XD, P.PE, P.VE, P.SE

    xd0 = xd.n
    xd.inc(16 * 3)

    @P.on("sync")
    def _(e):
        e.dma_start(AW72F.rearrange("p (c e) -> p c e", c=NCHUNK),
                    aw72_ap.rearrange("(c p) e -> p c e", p=128)).then_inc(xd.h, 16)
        e.dma_start(BFF, bflat_ap).then_inc(xd.h, 16)
        e.dma_start(BB, bpc_ap).then_inc(xd.h, 16)

    ve_prep0 = ve.n

    @P.on("vector")
    def _(e):
        e.wait_ge(xd.h, xd0 + 48)
        e.tensor_copy(AW72, AW72F)
        e.tensor_copy(BF, BFF).then_inc(ve.h, 1)
    ve_prep = ve.inc(1)
    P.barrier()

    for th in range(2):
        _emit_moe_half(P, ar, cst, ps, xin_ap, xout_ap, wba_bf, AW72, BF, BB, th)
    P.barrier()
    ar.pop()


def _emit_moe_half(P: Prog, ar: Arena, cst, ps, xin_ap, xout_ap, wba_bf,
                   AW72, BF, BB, th):
    nc = P.nc
    t0 = th * 1024
    xd, pe, ve, se = P.XD, P.PE, P.VE, P.SE
    P.barrier()
    ar.push()
    XF = ar.take(128, NCHUNK * 1024, BF16)
    emit_ln(P, ar, cst, ps, xin_ap, t0, 1024, XF)
    P.barrier()

    HT = ar.take(64, 1024, BF16)
    HP = [ar.take(128, 64, BF16) for _ in range(2)]
    SC = ar.take(128, 32, F32)
    M1 = SC[:, 0:1]
    M2 = SC[:, 1:2]
    D12 = SC[:, 2:3]
    G1 = SC[:, 3:4]
    G2 = SC[:, 4:5]
    IS1 = SC[:, 8:16]
    IS2 = SC[:, 16:24]
    W8 = SC[:, 24:32]
    L2 = ar.take(128, 8, F32)

    NT = 8
    pe_mm = [None] * NT
    pe_tr = [None] * NT
    ve_top = [None] * NT
    ve_cp = [None] * NT
    se_g = [None] * NT
    ve_mid = [None] * NT
    for i in range(NT):
        pe_mm[i] = pe.inc(1)
        if i >= 1:
            pe_tr[i - 1] = pe.inc(1)
    pe_tr[NT - 1] = pe.inc(1)
    for i in range(NT):
        se_g[i] = se.inc(2)
    v0 = ve.n
    vn = v0
    for i in range(NT):
        vn += 5
        ve_mid[i] = vn
        vn += 4
        vn += 1
        ve_top[i] = vn
        if i >= 1:
            vn += 1
            ve_cp[i - 1] = vn
    vn += 1
    ve_cp[NT - 1] = vn
    ve.n = vn

    @P.on("tensor")
    def _(e):
        ident = cst["ident"]
        for i in range(NT):
            bank = ps[i % 2]
            if i >= 2:
                e.wait_ge(ve.h, ve_top[i - 2])
            for c in range(NCHUNK):
                mm = e.matmul(bank.ap()[:, 0:72],
                              XF[:, c * 1024 + i * 128:c * 1024 + i * 128 + 128],
                              AW72[:, c * 72:(c + 1) * 72],
                              start=(c == 0), stop=(c == NCHUNK - 1))
            mm.then_inc(pe.h, 1)
            if i >= 1:
                j = i - 1
                tbank = ps[2 + j % 2].ap().bitcast(BF16)[0:64, 0:128]
                e.wait_ge(ve.h, ve_top[j])
                if j >= 2:
                    e.wait_ge(ve.h, ve_cp[j - 2])
                e.transpose(tbank, HP[j % 2], ident).then_inc(pe.h, 1)
        j = NT - 1
        tbank = ps[2 + j % 2].ap().bitcast(BF16)[0:64, 0:128]
        e.wait_ge(ve.h, ve_top[j])
        e.wait_ge(ve.h, ve_cp[j - 2])
        e.transpose(tbank, HP[j % 2], ident).then_inc(pe.h, 1)

    @P.on("scalar")
    def _(e):
        for i in range(NT):
            e.wait_ge(ve.h, ve_mid[i])
            if i >= 1:
                e.wait_ge(ve.h, ve_top[i - 1])
            e.activation(G1, D12, AF.Sigmoid).then_inc(se.h, 1)
            e.activation(G2, D12, AF.Sigmoid, scale=-1.0).then_inc(se.h, 1)

    @P.on("vector")
    def _(e):
        k = v0
        for i in range(NT):
            bank = ps[i % 2]
            lg = bank.ap()[:, 64:72]
            hx = bank.ap()[:, 0:64]
            e.wait_ge(pe.h, pe_mm[i])

            def step(ins):
                nonlocal k
                ins.then_inc(ve.h, 1)
                k += 1
                e.wait_ge(ve.h, k)
            if i >= 1:
                e.wait_ge(se.h, se_g[i - 1])
            step(e.reduce_max(M1, lg, XA))
            step(e.tensor_scalar(IS1, lg, M1, None, AX.is_equal))
            step(e.scalar_tensor_tensor(L2, IS1, -1e30, lg, AX.mult, AX.add))
            step(e.reduce_max(M2, L2, XA))
            step(e.tensor_sub(D12, M1, M2))
            e.wait_ge(se.h, se_g[i])
            step(e.tensor_scalar(W8, IS1, G1, 2.0, AX.mult, AX.mult))
            step(e.tensor_scalar(IS2, L2, M2, None, AX.is_equal))
            step(e.tensor_scalar(IS2, IS2, G2, 2.0, AX.mult, AX.mult))
            step(e.tensor_add(W8, W8, IS2))
            if i >= 2:
                e.wait_ge(pe.h, pe_tr[i - 2])
            step(e.tensor_tensor(
                HP[i % 2].rearrange("p (a b) -> p a b", a=8),
                hx.rearrange("p (a b) -> p a b", a=8),
                W8.unsqueeze(2).broadcast_to([128, 8, 8]),
                AX.mult))
            if i >= 1:
                j = i - 1
                e.wait_ge(pe.h, pe_tr[j])
                e.tensor_copy(HT[0:64, j * 128:(j + 1) * 128],
                              ps[2 + j % 2].ap().bitcast(BF16)[0:64, 0:128]).then_inc(ve.h, 1)
                k += 1
        j = NT - 1
        e.wait_ge(pe.h, pe_tr[j])
        e.tensor_copy(HT[0:64, j * 128:(j + 1) * 128],
                      ps[2 + j % 2].ap().bitcast(BF16)[0:64, 0:128]).then_inc(ve.h, 1)
        k += 1
        assert k == vn, (k, vn)

    P.barrier()

    WBB = [ar.take(128, NCHUNK * 128, BF16) for _ in range(2)]
    XT = [ar.take(128, 1024, F32) for _ in range(2)]
    wbapr = wba_bf.rearrange("(c p) o -> p c o", p=128)
    xint = xin_ap.rearrange("(c p) t -> c p t", p=128)
    xoutt = xout_ap.rearrange("(c p) t -> c p t", p=128)

    pe_b = pe.n
    ve_b = ve.n
    wb_tgt = [P.ws2.fill(oc, 1) for oc in range(NCHUNK)]
    xt_tgt = [P.xs.fill(oc, 1) for oc in range(NCHUNK)]
    st_base = xd.n
    xd.inc(16 * NCHUNK)

    @P.on("sync")
    def _(e):
        for oc in range(NCHUNK):
            b = oc % 2
            if oc >= 2:
                e.wait_ge(ve.h, ve_b + (oc - 2) * 2 + 2)
                e.dma_start(xoutt[oc - 2, :, t0:t0 + 1024], XT[b]).then_inc(xd.h, 16)
                e.wait_ge(pe.h, pe_b + (oc - 1) * 2)
            sh, _t = wb_tgt[oc]
            e.dma_start(WBB[b].rearrange("p (c w) -> p c w", w=128),
                        wbapr[:, :, oc * 128:(oc + 1) * 128]).then_inc(sh, 16)
            if oc >= 2:
                e.wait_ge(xd.h, st_base + 16 * (oc - 1))
            xh, _x = xt_tgt[oc]
            e.dma_start(XT[b], xint[oc, :, t0:t0 + 1024]).then_inc(xh, 16)
        for oc in (NCHUNK - 2, NCHUNK - 1):
            e.wait_ge(ve.h, ve_b + oc * 2 + 2)
            e.dma_start(xoutt[oc, :, t0:t0 + 1024], XT[oc % 2]).then_inc(xd.h, 16)
        e.wait_ge(xd.h, st_base + 16 * NCHUNK)

    @P.on("tensor")
    def _(e):
        for oc in range(NCHUNK):
            b = oc % 2
            e.wait_ge(*wb_tgt[oc])
            for tq in range(2):
                bank = (oc % 4) * 2 + tq
                if oc >= 4:
                    e.wait_ge(ve.h, ve_b + (oc - 4) * 2 + tq + 1)
                for c in range(NCHUNK):
                    e.matmul(ps[bank].ap(),
                             WBB[b][:, c * 128:(c + 1) * 128],
                             XF[:, c * 1024 + tq * 512:c * 1024 + tq * 512 + 512],
                             start=(c == 0), stop=False)
                mm = e.matmul(ps[bank].ap(),
                              BF[0:64, oc * 128:(oc + 1) * 128],
                              HT[0:64, tq * 512:tq * 512 + 512],
                              start=False, stop=True)
                mm.then_inc(pe.h, 1)
    pe.inc(NCHUNK * 2)

    @P.on("vector")
    def _(e):
        for oc in range(NCHUNK):
            b = oc % 2
            for tq in range(2):
                bank = (oc % 4) * 2 + tq
                e.wait_ge(pe.h, pe_b + oc * 2 + tq + 1)
                e.wait_ge(*xt_tgt[oc])
                e.scalar_tensor_tensor(XT[b][:, tq * 512:(tq + 1) * 512],
                                       ps[bank].ap(), BB[:, oc:oc + 1],
                                       XT[b][:, tq * 512:(tq + 1) * 512],
                                       AX.add, AX.add).then_inc(ve.h, 1)
    ve.inc(NCHUNK * 2)

    ar.pop()


def make_core_inputs(inputs, b):
    u = inputs["L0_attn_a_u"]
    d = {
        "xin": np.ascontiguousarray(u[b].T),
    }
    return d


def _ffn_weight_layout(inputs):
    return {
        "wqgT": np.ascontiguousarray(inputs["L0_ffn_wq_gate"].T),
        "s_gate": np.ascontiguousarray(inputs["L0_ffn_s_gate"]),
        "wquT": np.ascontiguousarray(inputs["L0_ffn_wq_up"].T),
        "s_up": np.ascontiguousarray(inputs["L0_ffn_s_up"]),
        "wqdT": np.ascontiguousarray(inputs["L0_ffn_wq_down"].T),
        "s_down": np.ascontiguousarray(inputs["L0_ffn_s_down"]),
    }


def build_ffn_test():
    nc = bass.Bass("TRN2", target_bir_lowering=False, debug=False)
    P = Prog(nc)
    xin = nc.dram_tensor("xin", [DM, T], F32, kind="ExternalInput").ap()
    xout = nc.dram_tensor("xout", [DM, T], F32, kind="ExternalOutput").ap()
    wqg = nc.dram_tensor("wqgT", [DM, HID], I32, kind="ExternalInput").ap()
    sg = nc.dram_tensor("s_gate", [HID], F32, kind="ExternalInput").ap()
    wqu = nc.dram_tensor("wquT", [DM, HID], I32, kind="ExternalInput").ap()
    su = nc.dram_tensor("s_up", [HID], F32, kind="ExternalInput").ap()
    wqd = nc.dram_tensor("wqdT", [HID, DM], I32, kind="ExternalInput").ap()
    sd = nc.dram_tensor("s_down", [DM], F32, kind="ExternalInput").ap()
    WG = nc.dram_tensor("WG_bf", [DM, HID], BF16, kind="Internal").ap()
    WU = nc.dram_tensor("WU_bf", [DM, HID], BF16, kind="Internal").ap()
    WDq = nc.dram_tensor("WD_bf", [HID, DM], BF16, kind="Internal").ap()

    ps = [nc.alloc_psum_tensor(f"ps{i}", [128, 512], F32) for i in range(8)]
    ar = Arena(nc, "AR", 192 * 1024)
    cst = Consts(nc, P, host_consts())

    emit_dequant(P, ar, cst, ps, wqg, sg, WG, DM, HID)
    emit_dequant(P, ar, cst, ps, wqu, su, WU, DM, HID)
    emit_dequant(P, ar, cst, ps, wqd, sd, WDq, HID, DM)
    P.barrier()
    emit_ffn(P, ar, cst, ps, xin, xout, WG, WU, WDq)
    P.finish()
    return nc


def emit_outproj_residual(P: Prog, ar: Arena, cst, ps, srcfn, w_ap, bias_pc,
                          xin_ap, xout_ap, nsrc=NCHUNK):
    nc = P.nc
    ar.push()
    WOB = [ar.take(128, nsrc * 128, BF16) for _ in range(2)]
    WOF = [ar.take(128, nsrc * 128, F32) for _ in range(2)]
    XT = [ar.take(128, T, F32) for _ in range(2)]
    wpr = w_ap.rearrange("(c p) o -> p c o", p=128)
    xint = xin_ap.rearrange("(c p) t -> c p t", p=128)
    xoutt = xout_ap.rearrange("(c p) t -> c p t", p=128)
    xd, pe, ve, ge = P.XD, P.PE, P.VE, P.GE
    NQ = 4

    pe_b = pe.n
    ve_b = ve.n
    ge_b = ge.n
    ge.inc(NCHUNK)
    w_tgt = [P.ws2.fill(oc, 1) for oc in range(NCHUNK)]
    xt_tgt = [P.xs.fill(oc, 1) for oc in range(NCHUNK)]
    st_base = xd.n
    xd.inc(16 * NCHUNK)

    @P.on("sync")
    def _(e):
        for oc in range(NCHUNK):
            b = oc % 2
            if oc >= 2:
                e.wait_ge(ve.h, ve_b + (oc - 2) * NQ + NQ)
                e.dma_start(xoutt[oc - 2], XT[b]).then_inc(xd.h, 16)
            if oc >= 2:
                e.wait_ge(ge.h, ge_b + oc - 1)
            sh, _t = w_tgt[oc]
            e.dma_start(WOF[b].rearrange("p (c w) -> p c w", w=128),
                        wpr[:, :, oc * 128:(oc + 1) * 128]).then_inc(sh, 16)
            if oc >= 2:
                e.wait_ge(xd.h, st_base + 16 * (oc - 1))
            xh, _x = xt_tgt[oc]
            e.dma_start(XT[b], xint[oc]).then_inc(xh, 16)
        for oc in (NCHUNK - 2, NCHUNK - 1):
            e.wait_ge(ve.h, ve_b + oc * NQ + NQ)
            e.dma_start(xoutt[oc], XT[oc % 2]).then_inc(xd.h, 16)
        e.wait_ge(xd.h, st_base + 16 * NCHUNK)

    @P.on(CAST_ENGINE)
    def _(e):
        for oc in range(NCHUNK):
            b = oc % 2
            e.wait_ge(*w_tgt[oc])
            if oc >= 2:
                e.wait_ge(pe.h, pe_b + (oc - 1) * NQ)
            e.tensor_copy(WOB[b], WOF[b]).then_inc(ge.h, 1)

    @P.on("tensor")
    def _(e):
        for oc in range(NCHUNK):
            b = oc % 2
            e.wait_ge(ge.h, ge_b + oc + 1)
            for tq in range(NQ):
                g = oc * NQ + tq
                bank = g % 8
                if g >= 8:
                    e.wait_ge(ve.h, ve_b + g - 8 + 1)
                for c in range(nsrc):
                    mm = e.matmul(ps[bank].ap(),
                                  WOB[b][:, c * 128:(c + 1) * 128],
                                  srcfn(c, tq * 512, 512),
                                  start=(c == 0), stop=(c == nsrc - 1))
                mm.then_inc(pe.h, 1)
    pe.inc(NCHUNK * NQ)

    @P.on("vector")
    def _(e):
        for oc in range(NCHUNK):
            b = oc % 2
            for tq in range(NQ):
                g = oc * NQ + tq
                bank = g % 8
                e.wait_ge(pe.h, pe_b + g + 1)
                e.wait_ge(*xt_tgt[oc])
                sl = XT[b][:, tq * 512:(tq + 1) * 512]
                if bias_pc is not None:
                    e.scalar_tensor_tensor(sl, ps[bank].ap(), bias_pc[:, oc:oc + 1],
                                           sl, AX.add, AX.add).then_inc(ve.h, 1)
                else:
                    e.tensor_tensor(sl, sl, ps[bank].ap(), AX.add).then_inc(ve.h, 1)
    ve.inc(NCHUNK * NQ)
    ar.pop()


def emit_gqa(P: Prog, ar: Arena, cst, ps, xin_ap, xout_ap, wq_ap, wk_ap,
             wv_ap, wo_ap, bpc_ap, scrq_ap, scrk_ap, scrv_ap, tbl_aps,
             dbg_ap=None, stage=3):
    nc = P.nc
    P.barrier()
    ar.push()
    xd, pe, ve, se = P.XD, P.PE, P.VE, P.SE

    BB = ar.take(128, NCHUNK, F32)
    xd0 = xd.n
    xd.inc(16)

    @P.on("sync")
    def _(e):
        e.dma_start(BB, bpc_ap).then_inc(xd.h, 16)
        e.wait_ge(xd.h, xd0 + 16)

    ar.push()
    XF = ar.take(128, NCHUNK * T, BF16)
    for th in range(2):
        P.barrier()
        emit_ln(P, ar, cst, ps, xin_ap, th * 1024, 1024, None,
                xf_fn=lambda c, o, th=th: XF[:, c * T + th * 1024 + o:
                                             c * T + th * 1024 + o + 512])
    P.barrier()

    if stage >= 1:
        _emit_gqa_proj(P, ar, cst, ps, XF, wq_ap, wk_ap, wv_ap,
                       scrq_ap, scrk_ap, scrv_ap, tbl_aps)
        P.barrier()
    ar.pop()

    if stage <= 1:
        if stage <= 0:
            ar2_dummy = None
        sb = ar.take(128, T, F32)
        xin_t = xin_ap.rearrange("(c p) t -> c p t", p=128)
        xout_t = xout_ap.rearrange("(c p) t -> c p t", p=128)
        xx0 = P.XD.n
        P.XD.inc(32 * NCHUNK)

        @P.on("sync")
        def _(e):
            for c2 in range(NCHUNK):
                e.dma_start(sb, xin_t[c2]).then_inc(P.XD.h, 16)
                e.wait_ge(P.XD.h, xx0 + 32 * c2 + 16)
                e.dma_start(xout_t[c2], sb).then_inc(P.XD.h, 16)
                e.wait_ge(P.XD.h, xx0 + 32 * c2 + 32)
        P.barrier()
        ar.pop()
        return

    ar.push()
    YT = ar.take(128, NCHUNK * T, BF16)
    _emit_gqa_core(P, ar, cst, ps, scrq_ap, scrk_ap, scrv_ap, YT)
    P.barrier()
    if dbg_ap is not None:
        dstm = P.XD.n
        P.XD.inc(16)

        @P.on("sync")
        def _(e):
            e.dma_start(dbg_ap.rearrange("(c p) t -> p c t", p=128),
                        YT.rearrange("p (c t) -> p c t", c=NCHUNK)).then_inc(P.XD.h, 16)
            e.wait_ge(P.XD.h, dstm + 16)
        P.barrier()

    if stage <= 2:
        sb = ar.take(128, T, F32)
        xin_t = xin_ap.rearrange("(c p) t -> c p t", p=128)
        xout_t = xout_ap.rearrange("(c p) t -> c p t", p=128)
        xx0 = P.XD.n
        P.XD.inc(32 * NCHUNK)

        @P.on("sync")
        def _(e):
            for c2 in range(NCHUNK):
                e.dma_start(sb, xin_t[c2]).then_inc(P.XD.h, 16)
                e.wait_ge(P.XD.h, xx0 + 32 * c2 + 16)
                e.dma_start(xout_t[c2], sb).then_inc(P.XD.h, 16)
                e.wait_ge(P.XD.h, xx0 + 32 * c2 + 32)
        P.barrier()
        ar.pop()
        ar.pop()
        return

    def srcfn(c, t0, tw):
        return YT[:, c * T + t0:c * T + t0 + tw]
    emit_outproj_residual(P, ar, cst, ps, srcfn, wo_ap, BB, xin_ap, xout_ap)
    P.barrier()
    ar.pop()
    ar.pop()


def _emit_gqa_proj(P: Prog, ar: Arena, cst, ps, XF, wq_ap, wk_ap, wv_ap,
                   scrq_ap, scrk_ap, scrv_ap, tbl_aps):
    nc = P.nc
    ar.push()
    xd, pe, ve, se = P.XD, P.PE, P.VE, P.SE
    TBL = {}
    tload = P.XD.n
    nt = len(tbl_aps)
    P.XD.inc(16 * nt)
    for name in tbl_aps:
        TBL[name] = ar.take(128, T, F32)

    @P.on("sync")
    def _(e):
        for name, ap_ in tbl_aps.items():
            e.dma_start(TBL[name], ap_).then_inc(P.XD.h, 16)
        e.wait_ge(P.XD.h, tload + 16 * nt)
    P.barrier()
    WB = [ar.take(128, NCHUNK * 128, BF16) for _ in range(2)]
    WBF = [ar.take(128, NCHUNK * 128, F32) for _ in range(2)]
    QR = [ar.take(128, 512, BF16) for _ in range(2)]
    TMPR = [ar.take(128, 512, F32) for _ in range(2)]
    ROPED = [ar.take(128, 512, BF16) for _ in range(2)]

    jobs = []
    for et in range(NCHUNK):
        jobs.append(("q", et))
    for et in range(4):
        jobs.append(("k", et))

    w_tgt = []
    for jj, (kind, et) in enumerate(jobs):
        w_tgt.append(P.ws.fill(jj, 1))
    st_tgt = [P.ss.fill(i, 1) for i in range(len(jobs) * 4)]
    ge = P.GE
    ge_b = ge.n
    ge.inc(len(jobs))

    pe_mm = {}
    pe_sw = {}
    se_c = {}
    ve_m = {}
    vn = ve.n
    sn = se.n
    for jj in range(len(jobs)):
        for tc in range(4):
            g = jj * 4 + tc
            pe_mm[g] = pe.inc(1)
            pe_sw[g] = pe.inc(1)
            se_c[g] = se.inc(1)
            vn += 3
            ve_m[g] = vn
    ve.n = vn

    @P.on("sync")
    def _(e):
        for jj, (kind, et) in enumerate(jobs):
            b = jj % 2
            wap = {"q": wq_ap, "k": wk_ap}[kind]
            wpr = wap.rearrange("(c p) o -> p c o", p=128)
            if jj >= 2:
                e.wait_ge(ge.h, ge_b + jj - 1)
            sh, _t = w_tgt[jj]
            e.dma_start(WBF[b].rearrange("p (c w) -> p c w", w=128),
                        wpr[:, :, et * 128:(et + 1) * 128]).then_inc(sh, 16)

    @P.on(CAST_ENGINE)
    def _(e):
        for jj, (kind, et) in enumerate(jobs):
            b = jj % 2
            e.wait_ge(*w_tgt[jj])
            if jj >= 2:
                e.wait_ge(pe.h, pe_mm[(jj - 1) * 4])
            e.tensor_copy(WB[b], WBF[b]).then_inc(ge.h, 1)

    @P.on("tensor")
    def _(e):
        sw = cst["swap32"]
        for jj, (kind, et) in enumerate(jobs):
            b = jj % 2
            e.wait_ge(ge.h, ge_b + jj + 1)
            for tc in range(4):
                g = jj * 4 + tc
                bank = ps[g % 2]
                bank2 = ps[2 + g % 2]
                if g >= 2:
                    e.wait_ge(ve.h, ve_m[g - 2])
                    e.wait_ge(se.h, se_c[g - 2])
                for c in range(NCHUNK):
                    mm = e.matmul(bank.ap(), WB[b][:, c * 128:(c + 1) * 128],
                                  XF[:, c * T + tc * 512:c * T + tc * 512 + 512],
                                  start=(c == 0), stop=(c == NCHUNK - 1))
                mm.then_inc(pe.h, 1)
                e.wait_ge(se.h, se_c[g])
                if g >= 2:
                    e.wait_ge(ve.h, ve_m[g - 2])
                if GQA_NO_SWAP:
                    e.nop().then_inc(pe.h, 1)
                else:
                    e.matmul(bank2.ap(), sw, QR[g % 2], start=True,
                             stop=True).then_inc(pe.h, 1)

    @P.on("scalar")
    def _(e):
        NG = len(jobs) * 4
        for jj, (kind, et) in enumerate(jobs):
            for tc in range(4):
                g = jj * 4 + tc
                bank = ps[g % 2]
                e.wait_ge(pe.h, pe_mm[g])
                if g >= 2:
                    e.wait_ge(pe.h, pe_sw[g - 2])
                e.activation(QR[g % 2], bank.ap(), AF.Copy).then_inc(se.h, 1)
                if g >= 1:
                    _store_roped(e, g - 1)
        _store_roped(e, NG - 1)
        e.wait_ge(*st_tgt[NG - 1])
        e.wait_ge(*st_tgt[NG - 2])

    def _store_roped(e, g):
        jj, tc = divmod(g, 4)
        kind, et = jobs[jj]
        e.wait_ge(ve.h, ve_m[g])
        dst = {"q": scrq_ap, "k": scrk_ap}[kind]
        sh, _t = st_tgt[g]
        e.dma_start(dst[et * 128:(et + 1) * 128, tc * 512:(tc + 1) * 512],
                    ROPED[g % 2]).then_inc(sh, 16)

    @P.on("vector")
    def _(e):
        k = ve.n
        k = ve_m[0] - 3
        for jj, (kind, et) in enumerate(jobs):
            for tc in range(4):
                g = jj * 4 + tc
                bank = ps[g % 2]
                bank2 = ps[2 + g % 2]
                cosv = TBL["gq_cos_q" if kind == "q" else "gq_cos_k"]
                sinv = TBL["gq_sin_q" if kind == "q" else "gq_sin_k"]
                e.wait_ge(pe.h, pe_mm[g])
                if g >= 2:
                    e.wait_ge(*st_tgt[g - 2])
                    e.wait_ge(ve.h, ve_m[g - 2])
                e.wait_ge(se.h, se_c[g])
                if GQA_SIMPLE_ROPE:
                    e.tensor_copy(TMPR[g % 2], bank.ap()).then_inc(ve.h, 1)
                    k += 1
                    e.wait_ge(pe.h, pe_sw[g])
                    e.wait_ge(ve.h, k)
                    e.nop().then_inc(ve.h, 1)
                    k += 1
                    e.wait_ge(ve.h, k)
                    e.tensor_copy(ROPED[g % 2], TMPR[g % 2]).then_inc(ve.h, 1)
                    k += 1
                    e.wait_ge(ve.h, k)
                    continue
                e.tensor_tensor(TMPR[g % 2], bank.ap(),
                                cosv[:, tc * 512:(tc + 1) * 512], AX.mult).then_inc(ve.h, 1)
                k += 1
                e.wait_ge(pe.h, pe_sw[g])
                e.wait_ge(ve.h, k)
                e.scalar_tensor_tensor(ROPED[g % 2], bank2.ap(), 1.0,
                                       sinv[:, tc * 512:(tc + 1) * 512],
                                       AX.mult, AX.mult).then_inc(ve.h, 1)
                k += 1
                e.wait_ge(ve.h, k)
                e.scalar_tensor_tensor(ROPED[g % 2], ROPED[g % 2], 1.0,
                                       TMPR[g % 2], AX.mult, AX.add).then_inc(ve.h, 1)
                k += 1
                e.wait_ge(ve.h, k)

    ar.pop()

    P.barrier()
    if GQA_NO_VPROJ:
        return
    ar.push()
    WVB = ar.take(128, NCHUNK * 512, BF16)
    WVF = ar.take(128, NCHUNK * 512, F32)
    VST = [ar.take(128, 8 * 65, BF16) for _ in range(2)]
    xd = P.XD
    xd1 = xd.n
    xd.inc(16)

    @P.on("sync")
    def _(e):
        e.dma_start(WVF.rearrange("p (c o) -> p c o", c=NCHUNK),
                    wv_ap.rearrange("(c p) o -> p c o", p=128)).then_inc(xd.h, 16)

    ve_v0 = ve.n

    @P.on("vector")
    def _(e):
        e.wait_ge(xd.h, xd1 + 16)
        e.tensor_copy(WVB, WVF).then_inc(ve.h, 1)
        e.memset(VST[0], 1.0)
        e.memset(VST[1], 1.0).then_inc(ve.h, 1)
    ve_vprep = ve.inc(2)

    pe_v0 = pe.n
    ve_v1 = ve.n
    st_v = [P.ss.fill(tb, 1) for tb in range(16)]

    @P.on("tensor")
    def _(e):
        e.wait_ge(ve.h, ve_vprep)
        for tb in range(16):
            bank = ps[4 + tb % 2]
            if tb >= 2:
                e.wait_ge(ve.h, ve_v1 + (tb - 2) + 1)
            for c in range(NCHUNK):
                mm = e.matmul(bank.ap(),
                              XF[:, c * T + tb * 128:c * T + tb * 128 + 128],
                              WVB[:, c * 512:(c + 1) * 512],
                              start=(c == 0), stop=(c == NCHUNK - 1))
            mm.then_inc(pe.h, 1)
    pe.inc(16)

    @P.on("vector")
    def _(e):
        for tb in range(16):
            b = tb % 2
            e.wait_ge(pe.h, pe_v0 + tb + 1)
            if tb >= 2:
                e.wait_ge(*st_v[tb - 2])
            e.tensor_copy(
                VST[b].rearrange("p (h o) -> p h o", h=8)[:, :, 0:64],
                ps[4 + b].ap().rearrange("p (h o) -> p h o", h=8)).then_inc(ve.h, 1)
    ve.inc(16)

    @P.on("sync")
    def _(e):
        for tb in range(16):
            b = tb % 2
            e.wait_ge(ve.h, ve_v1 + tb + 1)
            sh, _t = st_v[tb]
            e.dma_start(scrv_ap[tb * 128:(tb + 1) * 128, :], VST[b]).then_inc(sh, 16)
        e.wait_ge(*st_v[15])
        e.wait_ge(*st_v[14])
    ar.pop()


def _emit_gqa_core(P: Prog, ar: Arena, cst, ps, scrq_ap, scrk_ap, scrv_ap, YT):
    nc = P.nc
    ar.push()
    xd, pe, ve, se = P.XD, P.PE, P.VE, P.SE
    KT = [ar.take(64, T, BF16) for _ in range(2)]
    VT = [ar.take(128, 16 * 65, BF16) for _ in range(2)]
    QT = [ar.take(64, T, BF16) for _ in range(2)]
    PT = ar.take(128, 16 * 512, BF16)
    YB = ar.take(128, 512, BF16)
    RB = ar.take(128, 512, F32)

    kv_tgt = []
    q_tgt = []
    for kh in range(8):
        kv_tgt.append((P.ws.fill(kh, 1), P.ws2.fill(kh, 16)))
        for g in range(4):
            q_tgt.append(P.xs.fill(kh * 4 + g, 1))

    pe_s = {}
    se_e = {}
    ve_msk = {}
    pe_av = {}
    pe_nr = {}
    pe_bc = {}
    ve_radd = {}
    ve_rec = {}
    ve_cp = {}
    ve_mul = {}
    for qh in range(32):
        for tqc in range(4):
            for j in range(4 * tqc + 4):
                pe_s[(qh, tqc, j)] = pe.inc(1)
            pe_av[(qh, tqc)] = pe.inc(1)
            pe_nr[(qh, tqc)] = pe.inc(1)
    for qh in range(32):
        for tqc in range(4):
            for j in range(4 * tqc + 4):
                se_e[(qh, tqc, j)] = se.inc(1)
    vn = ve.n
    for qh in range(32):
        for tqc in range(4):
            for j in range(4 * tqc, 4 * tqc + 4):
                vn += 1
                ve_msk[(qh, tqc, j)] = vn
            vn += 1
            ve_radd[(qh, tqc)] = vn
            vn += 1
            ve_rec[(qh, tqc)] = vn
            vn += 1
            ve_cp[(qh, tqc)] = vn
            vn += 1
            ve_mul[(qh, tqc)] = vn
    ve.n = vn

    @P.on("sync")
    def _(e):
        for kh in range(8):
            kb = kh % 2
            if kh >= 2:
                e.wait_ge(pe.h, pe_nr[((kh - 2) * 4 + 3, 3)])
            (ksh, _kt), (vsh, _vt) = kv_tgt[kh]
            e.dma_start(KT[kb], scrk_ap[kh * 64:(kh + 1) * 64, :]).then_inc(ksh, 16)
            for tb in range(16):
                e.dma_start(VT[kb][:, tb * 65:(tb + 1) * 65],
                            scrv_ap[tb * 128:(tb + 1) * 128,
                                    kh * 65:(kh + 1) * 65]).then_inc(vsh, 16)
            for g in range(4):
                qh = kh * 4 + g
                qb = qh % 2
                if qh >= 2:
                    e.wait_ge(pe.h, pe_nr[(qh - 2, 3)])
                qsh, _qt = q_tgt[qh]
                e.dma_start(QT[qb], scrq_ap[qh * 64:(qh + 1) * 64, :]).then_inc(qsh, 16)

    @P.on("tensor")
    def _(e):
        onessq = cst["ones_sq"]
        for kh in range(8):
            kb = kh % 2
            (ksh, ktg), (vsh, vtg) = kv_tgt[kh]
            for g in range(4):
                qh = kh * 4 + g
                qb = qh % 2
                pb = (qh % 2) * 64
                e.wait_ge(ksh, ktg)
                e.wait_ge(vsh, vtg)
                e.wait_ge(*q_tgt[qh])
                for tqc in range(4):
                    for j in range(4 * tqc + 4):
                        tq0 = max(512 * tqc, 128 * j)
                        w = 512 * (tqc + 1) - tq0
                        bank = ps[(tqc * 4 + j) % 2]
                        pred = _prev_s(qh, tqc, j, 2)
                        if pred is not None:
                            e.wait_ge(se.h, se_e[pred])
                        e.matmul(bank.ap()[:, 0:w],
                                 KT[kb][:, j * 128:(j + 1) * 128],
                                 QT[qb][:, tq0:tq0 + w],
                                 start=True, stop=True).then_inc(pe.h, 1)
                    ybank = ps[4 + tqc % 2]
                    yb = ybank.ap()[pb:pb + 64, 0:512]
                    pc2 = _prev_chunk((qh, tqc), 2)
                    if pc2 is not None:
                        e.wait_ge(ve.h, ve_cp[pc2])
                    for j in range(4 * tqc + 4):
                        tq0 = max(512 * tqc, 128 * j)
                        w = 512 * (tqc + 1) - tq0
                        off = tq0 - 512 * tqc
                        if j >= 4 * tqc:
                            e.wait_ge(ve.h, ve_msk[(qh, tqc, j)])
                        else:
                            e.wait_ge(se.h, se_e[(qh, tqc, j)])
                        mm = e.matmul(yb[:, off:off + w],
                                      VT[kb][:, j * 65:j * 65 + 64],
                                      PT[:, j * 512:j * 512 + w],
                                      start=(j == 0), stop=(j == 4 * tqc + 3))
                    mm.then_inc(pe.h, 1)
                    pc1 = _prev_chunk((qh, tqc), 1)
                    if pc1 is not None:
                        e.wait_ge(ve.h, ve_radd[pc1])
                    for j in range(4 * tqc + 4):
                        tq0 = max(512 * tqc, 128 * j)
                        w = 512 * (tqc + 1) - tq0
                        off = tq0 - 512 * tqc
                        mm = e.matmul(ps[6].ap()[0:128, off:off + w],
                                      onessq, PT[:, j * 512:j * 512 + w],
                                      start=(j == 0), stop=(j == 4 * tqc + 3))
                    mm.then_inc(pe.h, 1)

    @P.on("scalar")
    def _(e):
        for qh in range(32):
            for tqc in range(4):
                for j in range(4 * tqc + 4):
                    tq0 = max(512 * tqc, 128 * j)
                    w = 512 * (tqc + 1) - tq0
                    bank = ps[(tqc * 4 + j) % 2]
                    e.wait_ge(pe.h, pe_s[(qh, tqc, j)])
                    pc1 = _prev_chunk((qh, tqc), 1)
                    if pc1 is not None:
                        e.wait_ge(pe.h, pe_nr[pc1])
                    e.activation(PT[:, j * 512:j * 512 + w], bank.ap()[:, 0:w],
                                 AF.Exp).then_inc(se.h, 1)

    @P.on("vector")
    def _(e):
        mask = cst["mask_ult"]
        for qh in range(32):
            pb = (qh % 2) * 64
            for tqc in range(4):
                for j in range(4 * tqc, 4 * tqc + 4):
                    e.wait_ge(se.h, se_e[(qh, tqc, j)])
                    e.tensor_tensor(PT[:, j * 512:j * 512 + 128],
                                    PT[:, j * 512:j * 512 + 128],
                                    mask, AX.mult).then_inc(ve.h, 1)
                e.wait_ge(pe.h, pe_nr[(qh, tqc)])
                pc1 = _prev_chunk((qh, tqc), 1)
                if pc1 is not None:
                    e.wait_ge(ve.h, ve_mul[pc1])
                e.tensor_scalar(RB, ps[6].ap()[0:128, 0:512], 1e-6, None,
                                AX.add).then_inc(ve.h, 1)
                e.wait_ge(ve.h, ve_radd[(qh, tqc)])
                e.reciprocal(RB, RB).then_inc(ve.h, 1)
                e.wait_ge(pe.h, pe_av[(qh, tqc)])
                if pc1 is not None:
                    e.wait_ge(ve.h, ve_mul[pc1])
                e.tensor_copy(YB[pb:pb + 64, 0:512],
                              ps[4 + tqc % 2].ap()[pb:pb + 64, 0:512]).then_inc(ve.h, 1)
                e.wait_ge(ve.h, ve_rec[(qh, tqc)])
                e.wait_ge(ve.h, ve_cp[(qh, tqc)])
                rt = qh // 2
                e.tensor_tensor(
                    YT[pb:pb + 64, rt * T + tqc * 512:rt * T + tqc * 512 + 512],
                    YB[pb:pb + 64, 0:512],
                    RB[pb:pb + 64, 0:512], AX.mult).then_inc(ve.h, 1)
    ar.pop()


def _prev_chunk(key, back):
    qh, tqc = key
    idx = qh * 4 + tqc - back
    if idx < 0:
        return None
    return (idx // 4, idx % 4)


def _prev_s(qh, tqc, j, back):
    seq = []
    for tq in range(4):
        for jj in range(4 * tq + 4):
            seq.append((tq, jj))
    idx = seq.index((tqc, j)) - back
    if idx >= 0:
        return (qh, seq[idx][0], seq[idx][1])
    if qh == 0:
        return None
    return (qh - 1, seq[idx][0], seq[idx][1])




def emit_swa(P: Prog, ar: Arena, cst, ps, xin_ap, xout_ap, wq_ap, wk_ap,
             wv_ap, wo_ap, scrq_ap, scrk_ap, scrv_ap):
    nc = P.nc
    P.barrier()
    ar.push()
    xd, pe, ve, se = P.XD, P.PE, P.VE, P.SE

    ar.push()
    XF = ar.take(128, NCHUNK * T, BF16)
    for th in range(2):
        P.barrier()
        emit_ln(P, ar, cst, ps, xin_ap, th * 1024, 1024, None,
                xf_fn=lambda c, o, th=th: XF[:, c * T + th * 1024 + o:
                                             c * T + th * 1024 + o + 512])
    P.barrier()

    _emit_swa_proj(P, ar, cst, ps, XF, wq_ap, wk_ap, wv_ap,
                   scrq_ap, scrk_ap, scrv_ap)
    P.barrier()
    ar.pop()

    ar.push()
    YT = ar.take(128, NCHUNK * T, BF16)
    _emit_swa_core(P, ar, cst, ps, scrq_ap, scrk_ap, scrv_ap, YT)
    P.barrier()

    def srcfn(c, t0, tw):
        return YT[:, c * T + t0:c * T + t0 + tw]
    emit_outproj_residual(P, ar, cst, ps, srcfn, wo_ap, None, xin_ap, xout_ap)
    P.barrier()
    ar.pop()
    ar.pop()


def _emit_swa_proj(P: Prog, ar: Arena, cst, ps, XF, wq_ap, wk_ap, wv_ap,
                   scrq_ap, scrk_ap, scrv_ap):
    nc = P.nc
    ar.push()
    xd, pe, ve, se = P.XD, P.PE, P.VE, P.SE
    WB = [ar.take(128, NCHUNK * 128, BF16) for _ in range(2)]
    WBF = [ar.take(128, NCHUNK * 128, F32) for _ in range(2)]
    OUTB = [ar.take(128, 512, BF16) for _ in range(2)]

    jobs = [("q", et) for et in range(NCHUNK)] + [("k", et) for et in range(NCHUNK)]
    w_tgt = [P.ws.fill(jj, 1) for jj in range(len(jobs))]
    st_tgt = [P.ss.fill(i, 1) for i in range(len(jobs) * 4)]
    ge = P.GE
    ge_b = ge.n
    ge.inc(len(jobs))
    pe_mm = {}
    se_c = {}
    for jj in range(len(jobs)):
        for tc in range(4):
            g = jj * 4 + tc
            pe_mm[g] = pe.inc(1)
            se_c[g] = se.inc(1)

    @P.on("sync")
    def _(e):
        for jj, (kind, et) in enumerate(jobs):
            b = jj % 2
            wap = {"q": wq_ap, "k": wk_ap}[kind]
            wpr = wap.rearrange("(c p) o -> p c o", p=128)
            if jj >= 2:
                e.wait_ge(ge.h, ge_b + jj - 1)
            sh, _t = w_tgt[jj]
            e.dma_start(WBF[b].rearrange("p (c w) -> p c w", w=128),
                        wpr[:, :, et * 128:(et + 1) * 128]).then_inc(sh, 16)

    @P.on("gpsimd")
    def _(e):
        for jj, (kind, et) in enumerate(jobs):
            b = jj % 2
            e.wait_ge(*w_tgt[jj])
            if jj >= 2:
                e.wait_ge(pe.h, pe_mm[(jj - 1) * 4])
            e.tensor_copy(WB[b], WBF[b]).then_inc(ge.h, 1)

    @P.on("tensor")
    def _(e):
        for jj, (kind, et) in enumerate(jobs):
            b = jj % 2
            e.wait_ge(ge.h, ge_b + jj + 1)
            for tc in range(4):
                g = jj * 4 + tc
                bank = ps[g % 2]
                if g >= 2:
                    e.wait_ge(se.h, se_c[g - 2])
                for c in range(NCHUNK):
                    mm = e.matmul(bank.ap(), WB[b][:, c * 128:(c + 1) * 128],
                                  XF[:, c * T + tc * 512:c * T + tc * 512 + 512],
                                  start=(c == 0), stop=(c == NCHUNK - 1))
                mm.then_inc(pe.h, 1)

    @P.on("scalar")
    def _(e):
        NG = len(jobs) * 4
        for jj, (kind, et) in enumerate(jobs):
            for tc in range(4):
                g = jj * 4 + tc
                bank = ps[g % 2]
                e.wait_ge(pe.h, pe_mm[g])
                if g >= 2:
                    e.wait_ge(*st_tgt[g - 2])
                if kind == "q":
                    e.activation(OUTB[g % 2], bank.ap(), AF.Copy,
                                 scale=cst["qscale_pc"]).then_inc(se.h, 1)
                else:
                    e.activation(OUTB[g % 2], bank.ap(), AF.Copy).then_inc(se.h, 1)
                e.wait_ge(se.h, se_c[g])
                dst = {"q": scrq_ap, "k": scrk_ap}[kind]
                sh, _t = st_tgt[g]
                e.dma_start(dst[et * 128:(et + 1) * 128, tc * 512:(tc + 1) * 512],
                            OUTB[g % 2]).then_inc(sh, 16)
        e.wait_ge(*st_tgt[NG - 1])
        e.wait_ge(*st_tgt[NG - 2])
    ar.pop()

    P.barrier()
    ar.push()
    WVB = [ar.take(128, NCHUNK * 512, BF16) for _ in range(2)]
    WVF = [ar.take(128, NCHUNK * 512, F32) for _ in range(2)]
    VST = [ar.take(128, 512, BF16) for _ in range(2)]
    ge = P.GE
    ge_b2 = ge.n
    ge.inc(4)
    wv_tgt = [P.ws2.fill(vg, 1) for vg in range(4)]
    stv_tgt = [P.ss.fill(i, 1) for i in range(64)]
    pe_v = {}
    se_v = {}
    for vg in range(4):
        for tb in range(16):
            pe_v[vg * 16 + tb] = pe.inc(1)
            se_v[vg * 16 + tb] = se.inc(1)

    @P.on("sync")
    def _(e):
        wvr = wv_ap.rearrange("(c p) o -> p c o", p=128)
        for vg in range(4):
            b = vg % 2
            if vg >= 2:
                e.wait_ge(ge.h, ge_b2 + vg - 1)
            sh, _t = wv_tgt[vg]
            e.dma_start(WVF[b].rearrange("p (c o) -> p c o", c=NCHUNK),
                        wvr[:, :, vg * 512:(vg + 1) * 512]).then_inc(sh, 16)

    @P.on("gpsimd")
    def _(e):
        for vg in range(4):
            b = vg % 2
            e.wait_ge(*wv_tgt[vg])
            if vg >= 2:
                e.wait_ge(pe.h, pe_v[(vg - 1) * 16])
            e.tensor_copy(WVB[b], WVF[b]).then_inc(ge.h, 1)

    @P.on("tensor")
    def _(e):
        for vg in range(4):
            b = vg % 2
            e.wait_ge(ge.h, ge_b2 + vg + 1)
            for tb in range(16):
                g = vg * 16 + tb
                bank = ps[4 + g % 2]
                if g >= 2:
                    e.wait_ge(se.h, se_v[g - 2])
                for c in range(NCHUNK):
                    mm = e.matmul(bank.ap(),
                                  XF[:, c * T + tb * 128:c * T + tb * 128 + 128],
                                  WVB[b][:, c * 512:(c + 1) * 512],
                                  start=(c == 0), stop=(c == NCHUNK - 1))
                mm.then_inc(pe.h, 1)

    @P.on("scalar")
    def _(e):
        for vg in range(4):
            for tb in range(16):
                g = vg * 16 + tb
                bank = ps[4 + g % 2]
                e.wait_ge(pe.h, pe_v[g])
                if g >= 2:
                    e.wait_ge(*stv_tgt[g - 2])
                e.activation(VST[g % 2], bank.ap(), AF.Copy).then_inc(se.h, 1)
                e.wait_ge(se.h, se_v[g])
                sh, _t = stv_tgt[g]
                e.dma_start(scrv_ap[tb * 128:(tb + 1) * 128,
                                    vg * 512:(vg + 1) * 512],
                            VST[g % 2]).then_inc(sh, 16)
        e.wait_ge(*stv_tgt[63])
        e.wait_ge(*stv_tgt[62])
    ar.pop()


def _emit_swa_core(P: Prog, ar: Arena, cst, ps, scrq_ap, scrk_ap, scrv_ap, YT):
    nc = P.nc
    ar.push()
    xd, pe, ve, se = P.XD, P.PE, P.VE, P.SE
    KT = [ar.take(128, T, BF16) for _ in range(2)]
    VT = [ar.take(128, 16 * 128, BF16) for _ in range(2)]
    QT = [ar.take(128, T, BF16) for _ in range(2)]
    PT = ar.take(128, 16 * 512, BF16)
    YB = ar.take(128, 512, BF16)
    RN = ar.take(1, 1024, F32)
    R0 = RN[0:1, 0:512]
    R1 = RN[0:1, 512:1024]

    q_tgt = []
    for h in range(16):
        q_tgt.append((P.ws.fill(h, 1), P.ws2.fill(h, 1), P.xs.fill(h, 1)))

    jsets = [_swa_jset(tqc) for tqc in range(4)]
    pe_s = {}
    se_e = {}
    ve_msk = {}
    pe_av = {}
    pe_nr = {}
    pe_bc = {}
    ve_radd = {}
    ve_rec = {}
    ve_cp = {}
    ve_mul = {}
    nchunks = 16 * 4
    for h in range(16):
        for tqc in range(4):
            for j in jsets[tqc]:
                pe_s[(h, tqc, j)] = pe.inc(1)
            cidx = h * 4 + tqc
            if cidx >= 1:
                pe_bc[_swa_pchunk(cidx - 1)] = pe.inc(1)
            pe_av[(h, tqc)] = pe.inc(1)
            pe_nr[(h, tqc)] = pe.inc(1)
    pe_bc[(15, 3)] = pe.inc(1)
    for h in range(16):
        for tqc in range(4):
            for j in jsets[tqc]:
                se_e[(h, tqc, j)] = se.inc(1)
    vn = ve.n
    for h in range(16):
        for tqc in range(4):
            for j in jsets[tqc]:
                for ii in range(4):
                    i = 4 * tqc + ii
                    if _swa_ij_included(i, j) and SWA_MIDX[(i, j)] is not None:
                        vn += 1
                        ve_msk[(h, tqc, j, i)] = vn
            vn += 1
            ve_radd[(h, tqc)] = vn
            vn += 1
            ve_rec[(h, tqc)] = vn
            vn += 1
            ve_cp[(h, tqc)] = vn
            vn += 1
            ve_mul[(h, tqc)] = vn
    ve.n = vn

    @P.on("sync")
    def _(e):
        for h in range(16):
            b = h % 2
            if h >= 2:
                e.wait_ge(pe.h, pe_nr[(h - 2, 3)])
            (ksh, _kt), (vsh, _vt), (qsh, _qt) = q_tgt[h]
            e.dma_start(KT[b], scrk_ap[h * 128:(h + 1) * 128, :]).then_inc(ksh, 16)
            e.dma_start(VT[b].rearrange("p (tb o) -> p tb o", tb=16),
                        scrv_ap.rearrange("(tb p) o -> p tb o", p=128)
                        [:, :, h * 128:(h + 1) * 128]).then_inc(vsh, 16)
            e.dma_start(QT[b], scrq_ap[h * 128:(h + 1) * 128, :]).then_inc(qsh, 16)

    @P.on("tensor")
    def _(e):
        onesb = cst["ones_col"]
        onesf = cst["ones_row_f32"]
        for h in range(16):
            b = h % 2
            (ksh, ktg), (vsh, vtg), (qsh, qtg) = q_tgt[h]
            e.wait_ge(ksh, ktg)
            e.wait_ge(vsh, vtg)
            e.wait_ge(qsh, qtg)
            for tqc in range(4):
                cidx = h * 4 + tqc
                js = jsets[tqc]
                for sn, j in enumerate(js):
                    bank = ps[_swa_sbank(h, tqc, sn)]
                    pred = _swa_prev_s(h, tqc, sn, 2, jsets)
                    if pred is not None:
                        e.wait_ge(se.h, se_e[pred])
                    e.matmul(bank.ap(), KT[b][:, j * 128:(j + 1) * 128],
                             QT[b][:, tqc * 512:(tqc + 1) * 512],
                             start=True, stop=True).then_inc(pe.h, 1)
                if cidx >= 1:
                    prev = _swa_pchunk(cidx - 1)
                    e.wait_ge(ve.h, ve_rec[prev])
                    pm = _swa_pchunk(cidx - 2) if cidx >= 2 else None
                    if pm is not None:
                        e.wait_ge(ve.h, ve_mul[pm])
                    e.matmul(ps[7].ap(), onesf, R1, start=True,
                             stop=True).then_inc(pe.h, 1)
                ybank = ps[4 + tqc % 2]
                pc2 = _swa_pchunk(cidx - 2) if cidx >= 2 else None
                if pc2 is not None:
                    e.wait_ge(ve.h, ve_cp[pc2])
                first_nm = True
                for ii in range(4):
                    i = 4 * tqc + ii
                    ijs = [j for j in js if _swa_ij_included(i, j)]
                    for jn, j in enumerate(ijs):
                        if SWA_MIDX[(i, j)] is not None:
                            e.wait_ge(ve.h, ve_msk[(h, tqc, j, i)])
                        else:
                            e.wait_ge(se.h, se_e[(h, tqc, j)])
                        mm = e.matmul(ybank.ap()[:, ii * 128:(ii + 1) * 128],
                                      VT[b][:, j * 128:(j + 1) * 128],
                                      PT[:, j * 512 + ii * 128:j * 512 + ii * 128 + 128],
                                      start=(i == 4 * tqc and jn == 0),
                                      stop=(ii == 3 and jn == len(ijs) - 1))
                mm.then_inc(pe.h, 1)
                pc1 = _swa_pchunk(cidx - 1) if cidx >= 1 else None
                if pc1 is not None:
                    e.wait_ge(ve.h, ve_radd[pc1])
                first = True
                for ii in range(4):
                    i = 4 * tqc + ii
                    ijs = [j for j in js if _swa_ij_included(i, j)]
                    for jn, j in enumerate(ijs):
                        mm = e.matmul(ps[6].ap()[0:1, ii * 128:(ii + 1) * 128],
                                      onesb,
                                      PT[:, j * 512 + ii * 128:j * 512 + ii * 128 + 128],
                                      start=first, stop=(ii == 3 and jn == len(ijs) - 1))
                        first = False
                mm.then_inc(pe.h, 1)
        e.wait_ge(ve.h, ve_rec[(15, 3)])
        e.wait_ge(ve.h, ve_mul[_swa_pchunk(62)])
        e.matmul(ps[7].ap(), cst["ones_row_f32"], R1, start=True,
                 stop=True).then_inc(pe.h, 1)

    @P.on("scalar")
    def _(e):
        for h in range(16):
            for tqc in range(4):
                js = jsets[tqc]
                for sn, j in enumerate(js):
                    bank = ps[_swa_sbank(h, tqc, sn)]
                    e.wait_ge(pe.h, pe_s[(h, tqc, j)])
                    cidx = h * 4 + tqc
                    pc1 = _swa_pchunk(cidx - 1) if cidx >= 1 else None
                    if pc1 is not None:
                        e.wait_ge(pe.h, pe_nr[pc1])
                    e.activation(PT[:, j * 512:(j + 1) * 512], bank.ap(),
                                 AF.Exp).then_inc(se.h, 1)

    @P.on("vector")
    def _(e):
        for h in range(16):
            for tqc in range(4):
                cidx = h * 4 + tqc
                js = jsets[tqc]
                for j in js:
                    for ii in range(4):
                        i = 4 * tqc + ii
                        if _swa_ij_included(i, j) and SWA_MIDX[(i, j)] is not None:
                            e.wait_ge(se.h, se_e[(h, tqc, j)])
                            sl = PT[:, j * 512 + ii * 128:j * 512 + ii * 128 + 128]
                            mi = SWA_MIDX[(i, j)]
                            e.tensor_tensor(sl, sl,
                                            cst["swa_masks"][:, mi * 128:(mi + 1) * 128],
                                            AX.mult).then_inc(ve.h, 1)
                e.wait_ge(pe.h, pe_nr[(h, tqc)])
                pc1 = _swa_pchunk(cidx - 1) if cidx >= 1 else None
                if pc1 is not None:
                    e.wait_ge(ve.h, ve_rec[pc1])
                e.tensor_scalar(R0, ps[6].ap()[0:1, 0:512], 1e-6, None,
                                AX.add).then_inc(ve.h, 1)
                e.wait_ge(ve.h, ve_radd[(h, tqc)])
                if pc1 is not None:
                    e.wait_ge(pe.h, pe_bc[pc1])
                e.reciprocal(R1, R0).then_inc(ve.h, 1)
                e.wait_ge(pe.h, pe_av[(h, tqc)])
                if pc1 is not None:
                    e.wait_ge(ve.h, ve_mul[pc1])
                e.tensor_copy(YB, ps[4 + tqc % 2].ap()).then_inc(ve.h, 1)
                e.wait_ge(pe.h, pe_bc[(h, tqc)])
                e.wait_ge(ve.h, ve_cp[(h, tqc)])
                e.tensor_tensor(
                    YT[:, h * T + tqc * 512:h * T + tqc * 512 + 512],
                    YB, ps[7].ap(), AX.mult).then_inc(ve.h, 1)
    ar.pop()


def _swa_pchunk(cidx):
    return (cidx // 4, cidx % 4)


def _swa_sbank(h, tqc, sn):
    tot = 0
    for t2 in range(tqc):
        tot += len(_swa_jset(t2))
    per_h = sum(len(_swa_jset(t2)) for t2 in range(4))
    return (h * per_h + tot + sn) % 2


def _swa_prev_s(h, tqc, sn, back, jsets):
    seq = []
    for t2 in range(4):
        for j2 in jsets[t2]:
            seq.append((t2, j2))
    per_h = len(seq)
    pos = 0
    for t2 in range(tqc):
        pos += len(jsets[t2])
    pos += sn
    gpos = h * per_h + pos - back
    if gpos < 0:
        return None
    h2, p2 = divmod(gpos, per_h)
    return (h2, seq[p2][0], seq[p2][1])


def emit_pow(P: Prog, ar: Arena, cst, ps, xin_ap, xout_ap, wq_ap, wk_ap,
             wv_ap, wg_ap, wc_ap, bpc_ap, scrq_ap, scrk_ap, scrv_ap,
             scr_eq_ap, scr_ek_ap):
    nc = P.nc
    P.barrier()
    ar.push()
    xd, pe, ve, se = P.XD, P.PE, P.VE, P.SE
    BB = ar.take(128, NCHUNK, F32)
    xd0 = xd.n
    xd.inc(16)

    @P.on("sync")
    def _(e):
        e.dma_start(BB, bpc_ap).then_inc(xd.h, 16)
        e.wait_ge(xd.h, xd0 + 16)

    ar.push()
    XF = ar.take(128, NCHUNK * T, BF16)
    for th in range(2):
        P.barrier()
        emit_ln(P, ar, cst, ps, xin_ap, th * 1024, 1024, None,
                xf_fn=lambda c, o, th=th: XF[:, c * T + th * 1024 + o:
                                             c * T + th * 1024 + o + 512])
    P.barrier()
    _emit_pow_gate(P, ar, cst, ps, XF, wg_ap, scr_eq_ap, scr_ek_ap)
    P.barrier()
    _emit_pow_proj(P, ar, cst, ps, XF, wq_ap, wk_ap, wv_ap,
                   scr_eq_ap, scr_ek_ap, scrq_ap, scrk_ap, scrv_ap)
    P.barrier()
    ar.pop()

    ar.push()
    YT = ar.take(128, NCHUNK * T, BF16)
    _emit_pow_core(P, ar, cst, ps, scrq_ap, scrk_ap, scrv_ap, YT)
    P.barrier()

    def srcfn(c, t0, tw):
        return YT[:, c * T + t0:c * T + t0 + tw]
    emit_outproj_residual(P, ar, cst, ps, srcfn, wc_ap, BB, xin_ap, xout_ap)
    P.barrier()
    ar.pop()
    ar.pop()


def _emit_pow_gate(P: Prog, ar: Arena, cst, ps, XF, wg_ap, scr_eq_ap, scr_ek_ap):
    nc = P.nc
    ar.push()
    xd, pe, ve, se = P.XD, P.PE, P.VE, P.SE
    WG = ar.take(128, NCHUNK * 8, BF16)
    WGF = ar.take(128, NCHUNK * 8, F32)
    GR = ar.take(8, T, F32)
    GB = ar.take(8, T, F32)
    EXPQ = ar.take(8, T, F32)
    EXPK = ar.take(8, T, F32)
    C8 = ar.take(8, 8, F32)
    ONES8 = ar.take(8, T, F32)
    EBST = [ar.take(128, 512, F32) for _ in range(2)]

    xd1 = xd.n
    xd.inc(16)

    @P.on("sync")
    def _(e):
        e.dma_start(WGF.rearrange("p (c o) -> p c o", c=NCHUNK),
                    wg_ap.rearrange("(c p) o -> p c o", p=128)).then_inc(xd.h, 16)

    ve0 = ve.n
    ve.inc(1)

    @P.on("vector")
    def _(e):
        e.wait_ge(xd.h, xd1 + 16)
        e.tensor_copy(WG, WGF)
        e.memset(ONES8, 1.0).then_inc(ve.h, 1)

    @P.on("tensor")
    def _(e):
        e.wait_ge(ve.h, ve0 + 1)
        for tc in range(4):
            for c in range(NCHUNK):
                mm = e.matmul(ps[tc].ap()[0:8, 0:512], WG[:, c * 8:(c + 1) * 8],
                              XF[:, c * T + tc * 512:c * T + tc * 512 + 512],
                              start=(c == 0), stop=(c == NCHUNK - 1))
        mm.then_inc(pe.h, 1)
    pe_g = pe.inc(1)

    ve1 = ve.n

    @P.on("vector")
    def _(e):
        e.wait_ge(pe.h, pe_g)
        for tc in range(4):
            e.tensor_copy(GR[:, tc * 512:(tc + 1) * 512],
                          ps[tc].ap()[0:8, 0:512]).then_inc(ve.h, 1)
        e.wait_ge(ve.h, ve_gr - 1)
        e.tensor_scalar(GB, GR, 6.906768, -1.0, AX.add, AX.mult).then_inc(ve.h, 1)
    ve_gr = ve.inc(5)

    se1 = se.n

    @P.on("scalar")
    def _(e):
        e.wait_ge(ve.h, ve_gr)
        e.activation(GR, GB, AF.Abs).then_inc(se.h, 1)
        e.wait_ge(se.h, se1 + 1)
        e.activation(GR, GR, AF.Exp, scale=-1.0).then_inc(se.h, 1)
        e.wait_ge(se.h, se1 + 2)
        e.activation(GR, GR, AF.Ln, bias=cst["ones_pc"][0:8, 0:1]).then_inc(se.h, 1)
    se_l1 = se.inc(3)

    ve2 = ve.n

    @P.on("vector")
    def _(e):
        k = ve2

        def step(ins):
            nonlocal k
            ins.then_inc(ve.h, 1)
            k += 1
            e.wait_ge(ve.h, k)
        e.wait_ge(se.h, se_l1)
        step(e.tensor_scalar(GB, GB, 0.0, None, AX.max))
        step(e.scalar_tensor_tensor(GB, GB, 1.0, GR, AX.mult, AX.add))
        step(e.tensor_scalar(GB, GB, -1.0, None, AX.mult))
        step(e.tensor_tensor_scan(GR, ONES8, GB, 0.0, AX.mult, AX.add))
        step(e.reduce_max(C8[:, 3:4], GR, XA))
        step(e.tensor_scalar(GB, GR, -1.0, None, AX.mult))
        step(e.reduce_max(C8[:, 4:5], GB, XA))
        step(e.tensor_sub(C8[:, 0:1], C8[:, 3:4], C8[:, 4:5]))
        step(e.tensor_scalar(C8[:, 0:1], C8[:, 0:1], 0.5, None, AX.mult))
        step(e.tensor_scalar(C8[:, 1:2], C8[:, 0:1], -0.5, None, AX.mult))
        step(e.tensor_scalar(C8[:, 2:3], C8[:, 0:1], 0.5, None, AX.mult))
    ve_c = ve.inc(11)

    @P.on("scalar")
    def _(e):
        e.wait_ge(ve.h, ve_c)
        e.activation(EXPQ, GR, AF.Exp, scale=cst["half_pc"][0:8, 0:1],
                     bias=C8[:, 1:2]).then_inc(se.h, 1)
        e.activation(EXPK, GR, AF.Exp, scale=cst["nhalf_pc"][0:8, 0:1],
                     bias=C8[:, 2:3]).then_inc(se.h, 1)
    se_eq = se.inc(2)

    ve3 = ve.n
    ve.inc(1)

    @P.on("vector")
    def _(e):
        e.wait_ge(se.h, se_eq)
        e.tensor_scalar(EXPQ, EXPQ, float(1.0 / np.sqrt(128.0)), None,
                        AX.mult).then_inc(ve.h, 1)
    ve_exq = ve3 + 1

    NB = 8 * 4 * 2
    pe_b = {}
    ve_b = {}
    vn = ve.n
    for i in range(NB):
        pe_b[i] = pe.inc(1)
        vn += 1
        ve_b[i] = vn
    ve.n = vn
    st_b = [P.ss.fill(i, 1) for i in range(NB)]

    @P.on("tensor")
    def _(e):
        sel = cst["sel8"]
        e.wait_ge(ve.h, ve_exq)
        for i in range(NB):
            gh, tc, qk = i // 8, (i // 2) % 4, i % 2
            grow = [EXPQ, EXPK][qk]
            bank = ps[4 + i % 2]
            if i >= 2:
                e.wait_ge(ve.h, ve_b[i - 2])
            e.matmul(bank.ap(), sel[0:8, gh * 128:(gh + 1) * 128],
                     grow[0:8, tc * 512:(tc + 1) * 512],
                     start=True, stop=True).then_inc(pe.h, 1)

    @P.on("vector")
    def _(e):
        for i in range(NB):
            b = i % 2
            e.wait_ge(pe.h, pe_b[i])
            if i >= 2:
                e.wait_ge(*st_b[i - 2])
            e.tensor_copy(EBST[b], ps[4 + b].ap()).then_inc(ve.h, 1)

    @P.on("sync")
    def _(e):
        for i in range(NB):
            b = i % 2
            gh, tc, qk = i // 8, (i // 2) % 4, i % 2
            dst = [scr_eq_ap, scr_ek_ap][qk]
            e.wait_ge(ve.h, ve_b[i])
            sh, _t = st_b[i]
            e.dma_start(dst[gh * 128:(gh + 1) * 128, tc * 512:(tc + 1) * 512],
                        EBST[b]).then_inc(sh, 16)
        e.wait_ge(*st_b[NB - 1])
        e.wait_ge(*st_b[NB - 2])
    ar.pop()


def _emit_pow_proj(P: Prog, ar: Arena, cst, ps, XF, wq_ap, wk_ap, wv_ap,
                   scr_eq_ap, scr_ek_ap, scrq_ap, scrk_ap, scrv_ap):
    nc = P.nc
    ar.push()
    xd, pe, ve, se = P.XD, P.PE, P.VE, P.SE
    WB = [ar.take(128, NCHUNK * 128, BF16) for _ in range(2)]
    WBF = [ar.take(128, NCHUNK * 128, F32) for _ in range(2)]
    EB = [ar.take(128, T, F32) for _ in range(2)]
    OUTB = [ar.take(128, 512, BF16) for _ in range(2)]

    jobs = [("q", et) for et in range(NCHUNK)] + [("k", et) for et in range(8)]
    w_tgt = [P.ws.fill(jj, 1) for jj in range(len(jobs))]
    eb_tgt = [P.xs.fill(jj, 1) for jj in range(len(jobs))]
    st_tgt = [P.ss.fill(i, 1) for i in range(len(jobs) * 4)]
    ge = P.GE
    ge_b = ge.n
    ge.inc(len(jobs))
    pe_mm = {}
    ve_o = {}
    vn = ve.n
    for jj in range(len(jobs)):
        for tc in range(4):
            g = jj * 4 + tc
            pe_mm[g] = pe.inc(1)
            vn += 1
            ve_o[g] = vn
    ve.n = vn

    @P.on("sync")
    def _(e):
        for jj, (kind, et) in enumerate(jobs):
            b = jj % 2
            wap = {"q": wq_ap, "k": wk_ap}[kind]
            wpr = wap.rearrange("(c p) o -> p c o", p=128)
            gh = (et // 2) if kind == "q" else et
            src_e = scr_eq_ap if kind == "q" else scr_ek_ap
            if jj >= 2:
                e.wait_ge(ge.h, ge_b + jj - 1)
                e.wait_ge(ve.h, ve_o[(jj - 2) * 4 + 3])
            sh, _t = w_tgt[jj]
            e.dma_start(WBF[b].rearrange("p (c w) -> p c w", w=128),
                        wpr[:, :, et * 128:(et + 1) * 128]).then_inc(sh, 16)
            eh, _t2 = eb_tgt[jj]
            e.dma_start(EB[b], src_e[gh * 128:(gh + 1) * 128, :]).then_inc(eh, 16)

    @P.on("gpsimd")
    def _(e):
        for jj, (kind, et) in enumerate(jobs):
            b = jj % 2
            e.wait_ge(*w_tgt[jj])
            if jj >= 2:
                e.wait_ge(pe.h, pe_mm[(jj - 1) * 4])
            e.tensor_copy(WB[b], WBF[b]).then_inc(ge.h, 1)

    @P.on("tensor")
    def _(e):
        for jj, (kind, et) in enumerate(jobs):
            b = jj % 2
            e.wait_ge(ge.h, ge_b + jj + 1)
            for tc in range(4):
                g = jj * 4 + tc
                bank = ps[g % 2]
                if g >= 2:
                    e.wait_ge(ve.h, ve_o[g - 2])
                for c in range(NCHUNK):
                    mm = e.matmul(bank.ap(), WB[b][:, c * 128:(c + 1) * 128],
                                  XF[:, c * T + tc * 512:c * T + tc * 512 + 512],
                                  start=(c == 0), stop=(c == NCHUNK - 1))
                mm.then_inc(pe.h, 1)

    @P.on("vector")
    def _(e):
        for jj, (kind, et) in enumerate(jobs):
            b = jj % 2
            e.wait_ge(*eb_tgt[jj])
            for tc in range(4):
                g = jj * 4 + tc
                bank = ps[g % 2]
                e.wait_ge(pe.h, pe_mm[g])
                if g >= 2:
                    e.wait_ge(*st_tgt[g - 2])
                e.tensor_tensor(OUTB[g % 2], bank.ap(),
                                EB[b][:, tc * 512:(tc + 1) * 512],
                                AX.mult).then_inc(ve.h, 1)

    @P.on("scalar")
    def _(e):
        NG = len(jobs) * 4
        for jj, (kind, et) in enumerate(jobs):
            for tc in range(4):
                g = jj * 4 + tc
                e.wait_ge(ve.h, ve_o[g])
                dst = {"q": scrq_ap, "k": scrk_ap}[kind]
                sh, _t = st_tgt[g]
                e.dma_start(dst[et * 128:(et + 1) * 128,
                                tc * 512:(tc + 1) * 512],
                            OUTB[g % 2]).then_inc(sh, 16)
        e.wait_ge(*st_tgt[NG - 1])
        e.wait_ge(*st_tgt[NG - 2])
    ar.pop()

    P.barrier()
    ar.push()
    WVB = [ar.take(128, NCHUNK * 512, BF16) for _ in range(2)]
    WVF = [ar.take(128, NCHUNK * 512, F32) for _ in range(2)]
    VST = [ar.take(128, 512, BF16) for _ in range(2)]
    ge = P.GE
    ge_b2 = ge.n
    ge.inc(2)
    wv_tgt = [P.ws2.fill(vg, 1) for vg in range(2)]
    stv_tgt = [P.ss.fill(i, 1) for i in range(32)]
    pe_v = {}
    se_v = {}
    for vg in range(2):
        for tb in range(16):
            pe_v[vg * 16 + tb] = pe.inc(1)
            se_v[vg * 16 + tb] = se.inc(1)

    @P.on("sync")
    def _(e):
        wvr = wv_ap.rearrange("(c p) o -> p c o", p=128)
        for vg in range(2):
            b = vg % 2
            sh, _t = wv_tgt[vg]
            e.dma_start(WVF[b].rearrange("p (c o) -> p c o", c=NCHUNK),
                        wvr[:, :, vg * 512:(vg + 1) * 512]).then_inc(sh, 16)

    @P.on("gpsimd")
    def _(e):
        for vg in range(2):
            b = vg % 2
            e.wait_ge(*wv_tgt[vg])
            e.tensor_copy(WVB[b], WVF[b]).then_inc(ge.h, 1)

    @P.on("tensor")
    def _(e):
        for vg in range(2):
            b = vg % 2
            e.wait_ge(ge.h, ge_b2 + vg + 1)
            for tb in range(16):
                g = vg * 16 + tb
                bank = ps[4 + g % 2]
                if g >= 2:
                    e.wait_ge(se.h, se_v[g - 2])
                for c in range(NCHUNK):
                    mm = e.matmul(bank.ap(),
                                  XF[:, c * T + tb * 128:c * T + tb * 128 + 128],
                                  WVB[b][:, c * 512:(c + 1) * 512],
                                  start=(c == 0), stop=(c == NCHUNK - 1))
                mm.then_inc(pe.h, 1)

    @P.on("scalar")
    def _(e):
        for vg in range(2):
            for tb in range(16):
                g = vg * 16 + tb
                bank = ps[4 + g % 2]
                e.wait_ge(pe.h, pe_v[g])
                if g >= 2:
                    e.wait_ge(*stv_tgt[g - 2])
                e.activation(VST[g % 2], bank.ap(), AF.Copy).then_inc(se.h, 1)
                e.wait_ge(se.h, se_v[g])
                sh, _t = stv_tgt[g]
                e.dma_start(scrv_ap[tb * 128:(tb + 1) * 128,
                                    vg * 512:(vg + 1) * 512],
                            VST[g % 2]).then_inc(sh, 16)
        e.wait_ge(*stv_tgt[31])
        e.wait_ge(*stv_tgt[30])
    ar.pop()


def _emit_pow_core(P: Prog, ar: Arena, cst, ps, scrq_ap, scrk_ap, scrv_ap, YT):
    nc = P.nc
    ar.push()
    xd, pe, ve, se = P.XD, P.PE, P.VE, P.SE
    KT = [ar.take(128, T, BF16) for _ in range(2)]
    VT = [ar.take(128, 16 * 128, BF16) for _ in range(2)]
    QT = [ar.take(128, T, BF16) for _ in range(2)]
    PT = ar.take(128, 16 * 512, BF16)
    YB = ar.take(128, 512, BF16)
    RN = ar.take(1, 1024, F32)
    R0 = RN[0:1, 0:512]
    R1 = RN[0:1, 512:1024]

    kv_tgt = []
    q_tgt = []
    for kh in range(8):
        kv_tgt.append((P.ws.fill(kh, 1), P.ws2.fill(kh, 1)))
        for g in range(2):
            q_tgt.append(P.xs.fill(kh * 2 + g, 1))

    NQH = 16
    pe_s = {}
    se_e = {}
    ve_msk = {}
    pe_av = {}
    pe_nr = {}
    pe_bc = {}
    ve_radd = {}
    ve_rec = {}
    ve_cp = {}
    ve_mul = {}
    for qh in range(NQH):
        for tqc in range(4):
            for j in range(4 * tqc + 4):
                pe_s[(qh, tqc, j)] = pe.inc(1)
            cidx = qh * 4 + tqc
            if cidx >= 1:
                pe_bc[_prev_chunk((qh, tqc), 1)] = pe.inc(1)
            pe_av[(qh, tqc)] = pe.inc(1)
            pe_nr[(qh, tqc)] = pe.inc(1)
    pe_bc[(NQH - 1, 3)] = pe.inc(1)
    for qh in range(NQH):
        for tqc in range(4):
            for j in range(4 * tqc + 4):
                se_e[(qh, tqc, j)] = se.inc(1)
    vn = ve.n
    for qh in range(NQH):
        for tqc in range(4):
            for j in range(4 * tqc, 4 * tqc + 4):
                vn += 1
                ve_msk[(qh, tqc, j)] = vn
            vn += 1
            ve_radd[(qh, tqc)] = vn
            vn += 1
            ve_rec[(qh, tqc)] = vn
            vn += 1
            ve_cp[(qh, tqc)] = vn
            vn += 1
            ve_mul[(qh, tqc)] = vn
    ve.n = vn

    @P.on("sync")
    def _(e):
        for kh in range(8):
            kb = kh % 2
            if kh >= 2:
                e.wait_ge(pe.h, pe_nr[((kh - 2) * 2 + 1, 3)])
            (ksh, _kt), (vsh, _vt) = kv_tgt[kh]
            e.dma_start(KT[kb], scrk_ap[kh * 128:(kh + 1) * 128, :]).then_inc(ksh, 16)
            e.dma_start(VT[kb].rearrange("p (tb o) -> p tb o", tb=16),
                        scrv_ap.rearrange("(tb p) o -> p tb o", p=128)
                        [:, :, kh * 128:(kh + 1) * 128]).then_inc(vsh, 16)
            for g in range(2):
                qh = kh * 2 + g
                qb = qh % 2
                if qh >= 2:
                    e.wait_ge(pe.h, pe_nr[(qh - 2, 3)])
                qsh, _qt = q_tgt[qh]
                e.dma_start(QT[qb], scrq_ap[qh * 128:(qh + 1) * 128, :]).then_inc(qsh, 16)

    @P.on("tensor")
    def _(e):
        onesb = cst["ones_col"]
        onesf = cst["ones_row_f32"]
        for kh in range(8):
            kb = kh % 2
            (ksh, ktg), (vsh, vtg) = kv_tgt[kh]
            for g in range(2):
                qh = kh * 2 + g
                qb = qh % 2
                e.wait_ge(ksh, ktg)
                e.wait_ge(vsh, vtg)
                e.wait_ge(*q_tgt[qh])
                for tqc in range(4):
                    for j in range(4 * tqc + 4):
                        tq0 = max(512 * tqc, 128 * j)
                        w = 512 * (tqc + 1) - tq0
                        bank = ps[(tqc * 4 + j) % 2]
                        pred = _prev_s(qh, tqc, j, 2)
                        if pred is not None:
                            e.wait_ge(se.h, se_e[pred])
                        e.matmul(bank.ap()[:, 0:w],
                                 KT[kb][:, j * 128:(j + 1) * 128],
                                 QT[qb][:, tq0:tq0 + w],
                                 start=True, stop=True).then_inc(pe.h, 1)
                    if tqc >= 1:
                        prev = (qh, tqc - 1)
                    elif qh >= 1:
                        prev = (qh - 1, 3)
                    else:
                        prev = None
                    if prev is not None:
                        e.wait_ge(ve.h, ve_rec[prev])
                        pm = _prev_chunk(prev, 1)
                        if pm is not None:
                            e.wait_ge(ve.h, ve_mul[pm])
                        e.matmul(ps[7].ap(), onesf, R1,
                                 start=True, stop=True).then_inc(pe.h, 1)
                    ybank = ps[4 + tqc % 2]
                    pc2 = _prev_chunk((qh, tqc), 2)
                    if pc2 is not None:
                        e.wait_ge(ve.h, ve_cp[pc2])
                    for j in range(4 * tqc + 4):
                        tq0 = max(512 * tqc, 128 * j)
                        w = 512 * (tqc + 1) - tq0
                        off = tq0 - 512 * tqc
                        if j >= 4 * tqc:
                            e.wait_ge(ve.h, ve_msk[(qh, tqc, j)])
                        else:
                            e.wait_ge(se.h, se_e[(qh, tqc, j)])
                        mm = e.matmul(ybank.ap()[:, off:off + w],
                                      VT[kb][:, j * 128:(j + 1) * 128],
                                      PT[:, j * 512:j * 512 + w],
                                      start=(j == 0), stop=(j == 4 * tqc + 3))
                    mm.then_inc(pe.h, 1)
                    pc1 = _prev_chunk((qh, tqc), 1)
                    if pc1 is not None:
                        e.wait_ge(ve.h, ve_radd[pc1])
                    for j in range(4 * tqc + 4):
                        tq0 = max(512 * tqc, 128 * j)
                        w = 512 * (tqc + 1) - tq0
                        off = tq0 - 512 * tqc
                        mm = e.matmul(ps[6].ap()[0:1, off:off + w],
                                      onesb, PT[:, j * 512:j * 512 + w],
                                      start=(j == 0), stop=(j == 4 * tqc + 3))
                    mm.then_inc(pe.h, 1)
        e.wait_ge(ve.h, ve_rec[(NQH - 1, 3)])
        e.wait_ge(ve.h, ve_mul[_prev_chunk((NQH - 1, 3), 1)])
        e.matmul(ps[7].ap(), cst["ones_row_f32"], R1,
                 start=True, stop=True).then_inc(pe.h, 1)

    @P.on("scalar")
    def _(e):
        for qh in range(NQH):
            for tqc in range(4):
                for j in range(4 * tqc + 4):
                    tq0 = max(512 * tqc, 128 * j)
                    w = 512 * (tqc + 1) - tq0
                    bank = ps[(tqc * 4 + j) % 2]
                    e.wait_ge(pe.h, pe_s[(qh, tqc, j)])
                    pc1 = _prev_chunk((qh, tqc), 1)
                    if pc1 is not None:
                        e.wait_ge(pe.h, pe_nr[pc1])
                    e.activation(PT[:, j * 512:j * 512 + w], bank.ap()[:, 0:w],
                                 AF.Square).then_inc(se.h, 1)

    @P.on("vector")
    def _(e):
        mask = cst["mask_ult"]
        for qh in range(NQH):
            for tqc in range(4):
                for j in range(4 * tqc, 4 * tqc + 4):
                    e.wait_ge(se.h, se_e[(qh, tqc, j)])
                    e.tensor_tensor(PT[:, j * 512:j * 512 + 128],
                                    PT[:, j * 512:j * 512 + 128],
                                    mask, AX.mult).then_inc(ve.h, 1)
                e.wait_ge(pe.h, pe_nr[(qh, tqc)])
                pc1 = _prev_chunk((qh, tqc), 1)
                if pc1 is not None:
                    e.wait_ge(ve.h, ve_rec[pc1])
                e.tensor_scalar(R0, ps[6].ap()[0:1, 0:512], 1e-6, None,
                                AX.add).then_inc(ve.h, 1)
                e.wait_ge(ve.h, ve_radd[(qh, tqc)])
                if pc1 is not None:
                    e.wait_ge(pe.h, pe_bc[pc1])
                e.reciprocal(R1, R0).then_inc(ve.h, 1)
                e.wait_ge(pe.h, pe_av[(qh, tqc)])
                if pc1 is not None:
                    e.wait_ge(ve.h, ve_mul[pc1])
                e.tensor_copy(YB, ps[4 + tqc % 2].ap()).then_inc(ve.h, 1)
                e.wait_ge(pe.h, pe_bc[(qh, tqc)])
                e.wait_ge(ve.h, ve_cp[(qh, tqc)])
                e.tensor_tensor(
                    YT[:, qh * T + tqc * 512:qh * T + tqc * 512 + 512],
                    YB, ps[7].ap(), AX.mult).then_inc(ve.h, 1)
    ar.pop()


def _swa_weight_layout(inputs):
    return {
        "sw_wq": np.ascontiguousarray(inputs["L1_attn_a_Wq"]),
        "sw_wk": np.ascontiguousarray(inputs["L1_attn_a_Wk"]),
        "sw_wv": np.ascontiguousarray(inputs["L1_attn_a_Wv"]),
        "sw_wo": np.ascontiguousarray(inputs["L1_attn_a_Wo"]),
    }


def build_swa_test():
    nc = bass.Bass("TRN2", target_bir_lowering=False, debug=False)
    P = Prog(nc)
    xin = nc.dram_tensor("xin", [DM, T], F32, kind="ExternalInput").ap()
    xout = nc.dram_tensor("xout", [DM, T], F32, kind="ExternalOutput").ap()
    wq = nc.dram_tensor("sw_wq", [DM, DM], F32, kind="ExternalInput").ap()
    wk = nc.dram_tensor("sw_wk", [DM, DM], F32, kind="ExternalInput").ap()
    wv = nc.dram_tensor("sw_wv", [DM, DM], F32, kind="ExternalInput").ap()
    wo = nc.dram_tensor("sw_wo", [DM, DM], F32, kind="ExternalInput").ap()
    scrq = nc.dram_tensor("scrq", [DM, T], BF16, kind="Internal").ap()
    scrk = nc.dram_tensor("scrk", [DM, T], BF16, kind="Internal").ap()
    scrv = nc.dram_tensor("scrv", [T, DM], BF16, kind="Internal").ap()

    ps = [nc.alloc_psum_tensor(f"ps{i}", [128, 512], F32) for i in range(8)]
    ar = Arena(nc, "AR", 192 * 1024)
    cst = Consts(nc, P, host_consts())

    emit_swa(P, ar, cst, ps, xin, xout, wq, wk, wv, wo, scrq, scrk, scrv)
    P.finish()
    return nc


def emit_mamba(P: Prog, ar: Arena, cst, ps, xin_ap, xout_ap, winT_ap, woutT_ap,
               cw_ap, cb_ap, dtb_ap, alog_ap, drow_ap, nwrow_ap,
               scr_xc_ap, scr_z_ap, scr_yn_ap):
    nc = P.nc
    P.barrier()
    ar.push()
    xd, pe, ve, se = P.XD, P.PE, P.VE, P.SE

    BT = ar.take(128, T, BF16)
    CT = ar.take(128, T, BF16)
    DTS = ar.take(64, T, F32)
    LAM = ar.take(64, T, F32)
    SM = ar.take(128, NCHUNK, F32)
    NWB = ar.take(128, 4096, F32)
    DBC = ar.take(128, 64, F32)

    ar.push()
    XF = ar.take(128, NCHUNK * T, BF16)
    for th in range(2):
        P.barrier()
        emit_ln(P, ar, cst, ps, xin_ap, th * 1024, 1024, None,
                xf_fn=lambda c, o, th=th: XF[:, c * T + th * 1024 + o:
                                             c * T + th * 1024 + o + 512])
    P.barrier()
    _emit_mamba_prep(P, ar, cst, ps, dtb_ap, alog_ap, drow_ap, nwrow_ap,
                     SM, NWB, DBC)
    P.barrier()
    _emit_mamba_zproj(P, ar, cst, ps, XF, winT_ap, scr_z_ap)
    P.barrier()
    _emit_mamba_m1(P, ar, cst, ps, XF, winT_ap, cw_ap, cb_ap, BT, CT, DTS,
                   scr_xc_ap)
    P.barrier()
    ar.pop()
    _emit_mamba_dt(P, ar, cst, ps, DTS, LAM, SM)
    P.barrier()
    _emit_mamba_m2(P, ar, cst, ps, BT, CT, DTS, LAM, NWB, DBC,
                   scr_xc_ap, scr_z_ap, scr_yn_ap)
    P.barrier()
    ar.pop()
    ar.push()
    _emit_mamba_outproj(P, ar, cst, ps, woutT_ap, scr_yn_ap, xin_ap, xout_ap)
    P.barrier()
    ar.pop()


def _emit_mamba_prep(P: Prog, ar: Arena, cst, ps, dtb_ap, alog_ap, drow_ap,
                     nwrow_ap, SM, NWB, DBC):
    nc = P.nc
    ar.push()
    xd, pe, ve, se = P.XD, P.PE, P.VE, P.SE
    RW = ar.take(1, 4096 + 64, F32)
    xd0 = xd.n
    xd.inc(16 * 4)

    @P.on("sync")
    def _(e):
        e.dma_start(SM[:, 0:1], dtb_ap).then_inc(xd.h, 16)
        e.dma_start(SM[:, 1:2], alog_ap).then_inc(xd.h, 16)
        e.dma_start(RW[0:1, 0:4096], nwrow_ap).then_inc(xd.h, 16)
        e.dma_start(RW[0:1, 4096:4160], drow_ap).then_inc(xd.h, 16)
        e.wait_ge(xd.h, xd0 + 64)

    pe0 = pe.n
    ve0 = ve.n

    @P.on("tensor")
    def _(e):
        onesf = cst["ones_row_f32"]
        e.wait_ge(xd.h, xd0 + 64)
        for q in range(8):
            if q >= 4:
                e.wait_ge(ve.h, ve0 + q - 3)
            e.matmul(ps[q % 4].ap(), onesf, RW[0:1, q * 512:(q + 1) * 512],
                     start=True, stop=True).then_inc(pe.h, 1)
        e.matmul(ps[5].ap()[:, 0:64], onesf, RW[0:1, 4096:4160],
                 start=True, stop=True).then_inc(pe.h, 1)
    pe.inc(9)

    @P.on("vector")
    def _(e):
        for q in range(8):
            e.wait_ge(pe.h, pe0 + q + 1)
            e.tensor_copy(NWB[:, q * 512:(q + 1) * 512],
                          ps[q % 4].ap()).then_inc(ve.h, 1)
        e.wait_ge(pe.h, pe0 + 9)
        e.tensor_copy(DBC, ps[5].ap()[:, 0:64]).then_inc(ve.h, 1)
    ve.inc(9)

    se0 = se.n

    @P.on("scalar")
    def _(e):
        e.wait_ge(xd.h, xd0 + 64)
        e.activation(SM[:, 2:3], SM[:, 1:2], AF.Exp).then_inc(se.h, 1)
    se_a = se.inc(1)

    ve1 = ve.n
    ve.inc(1)

    @P.on("vector")
    def _(e):
        e.wait_ge(se.h, se_a)
        e.tensor_scalar(SM[:, 2:3], SM[:, 2:3], -1.0, None,
                        AX.mult).then_inc(ve.h, 1)
    ar.pop()


def _emit_mamba_zproj(P: Prog, ar: Arena, cst, ps, XF, winT_ap, scr_z_ap):
    nc = P.nc
    ar.push()
    xd, pe, ve, se = P.XD, P.PE, P.VE, P.SE
    NZ = 8
    ZW = 512
    WZB = [ar.take(128, NCHUNK * ZW, BF16) for _ in range(2)]
    WZF = ar.take(128, NCHUNK * ZW // 2, F32)
    VST = [ar.take(128, ZW, BF16) for _ in range(2)]
    ge = P.GE
    ge_b = ge.n
    ge.inc(NZ * 2)
    wz_tgt = [P.ws2.fill(i % 16, 1) for i in range(NZ * 2)]
    st_tgt = [P.ss.fill(i, 1) for i in range(NZ * 16)]
    pe_v = {}
    se_v = {}
    for g in range(NZ * 16):
        pe_v[g] = pe.inc(1)
        se_v[g] = se.inc(1)

    @P.on("sync")
    def _(e):
        wzr = winT_ap.rearrange("(c p) o -> p c o", p=128)
        for zg in range(NZ):
            for h in range(2):
                i = 2 * zg + h
                if i >= 1:
                    e.wait_ge(ge.h, ge_b + i)
                sh, _t = wz_tgt[i]
                e.dma_start(WZF.rearrange("p (c o) -> p c o", c=NCHUNK // 2),
                            wzr[:, h * 8:h * 8 + 8,
                                zg * ZW:(zg + 1) * ZW]).then_inc(sh, 16)

    @P.on("gpsimd")
    def _(e):
        for zg in range(NZ):
            b = zg % 2
            for h in range(2):
                i = 2 * zg + h
                e.wait_ge(*wz_tgt[i])
                if h == 0 and zg >= 2:
                    e.wait_ge(pe.h, pe_v[(zg - 1) * 16])
                e.tensor_copy(WZB[b][:, h * 8 * ZW:(h + 1) * 8 * ZW],
                              WZF).then_inc(ge.h, 1)

    @P.on("tensor")
    def _(e):
        for zg in range(NZ):
            b = zg % 2
            e.wait_ge(ge.h, ge_b + 2 * zg + 2)
            for tb in range(16):
                g = zg * 16 + tb
                bank = ps[4 + g % 2]
                if g >= 2:
                    e.wait_ge(se.h, se_v[g - 2])
                for c in range(NCHUNK):
                    mm = e.matmul(bank.ap()[:, 0:ZW],
                                  XF[:, c * T + tb * 128:c * T + tb * 128 + 128],
                                  WZB[b][:, c * ZW:(c + 1) * ZW],
                                  start=(c == 0), stop=(c == NCHUNK - 1))
                mm.then_inc(pe.h, 1)

    @P.on("scalar")
    def _(e):
        for zg in range(NZ):
            for tb in range(16):
                g = zg * 16 + tb
                bank = ps[4 + g % 2]
                e.wait_ge(pe.h, pe_v[g])
                if g >= 2:
                    e.wait_ge(*st_tgt[g - 2])
                e.activation(VST[g % 2], bank.ap()[:, 0:ZW],
                             AF.Copy).then_inc(se.h, 1)
                e.wait_ge(se.h, se_v[g])
                sh, _t = st_tgt[g]
                e.dma_start(scr_z_ap[tb * 128:(tb + 1) * 128,
                                     zg * ZW:(zg + 1) * ZW],
                            VST[g % 2]).then_inc(sh, 16)
        e.wait_ge(*st_tgt[NZ * 16 - 1])
        e.wait_ge(*st_tgt[NZ * 16 - 2])
    ar.pop()


def _emit_mamba_m1(P: Prog, ar: Arena, cst, ps, XF, winT_ap, cw_ap, cb_ap,
                   BT, CT, DTS, scr_xc_ap):
    nc = P.nc
    ar.push()
    xd, pe, ve, se = P.XD, P.PE, P.VE, P.SE
    WB = [ar.take(128, NCHUNK * 128, BF16) for _ in range(2)]
    WBF = [ar.take(128, NCHUNK * 128, F32) for _ in range(2)]
    CWPC = ar.take(128, 34 * 4, F32)
    CBPC = ar.take(128, 34, F32)
    XBCR = ar.take(128, T + 4, F32)
    CONVF = ar.take(128, T, F32)
    CONV = ar.take(128, T, BF16)
    XCST = [ar.take(128, NCHUNK * 128, BF16) for _ in range(2)]

    winr = winT_ap.rearrange("(c p) o -> c p o", p=128)
    xd0 = xd.n
    xd.inc(32)

    @P.on("sync")
    def _(e):
        e.dma_start(CWPC, cw_ap).then_inc(xd.h, 16)
        e.dma_start(CBPC, cb_ap).then_inc(xd.h, 16)
        e.wait_ge(xd.h, xd0 + 32)

    jobs = list(range(35))
    w_tgt = [P.ws.fill(jj, 1) for jj in jobs]
    st_x = {}
    ge = P.GE
    ge_b = ge.n
    ge.inc(len(jobs))
    pe_mm = {}
    pe_tr = {}
    se_cv = {}
    ve_ev = {}
    ve_cv = {}
    ve_tre = {}
    ve_bc = {}
    for jj in jobs:
        for tc in range(4):
            pe_mm[(jj, tc)] = pe.inc(1)
        if jj < 32:
            for tb in range(16):
                pe_tr[(jj, tb)] = pe.inc(1)
    for jj in jobs:
        if jj < 34:
            se_cv[jj] = se.inc(1)
    vn = ve.n + 1
    ve_pad = vn
    for jj in jobs:
        for tc in range(4):
            vn += 1
            ve_ev[(jj, tc)] = vn
        if jj < 34:
            vn += 4
            ve_cv[jj] = vn
        if jj < 32:
            for tb in range(16):
                vn += 1
                ve_tre[(jj, tb)] = vn
            st_x[jj] = P.ss.fill(jj, 1)
        elif jj < 34:
            vn += 1
            ve_bc[jj] = vn
        else:
            pass
    ve.n = vn

    @P.on("sync")
    def _(e):
        winpr = winT_ap.rearrange("(c p) o -> p c o", p=128)
        for jj in jobs:
            b = jj % 2
            col0 = 4096 + jj * 128
            ncols = 64 if jj == 34 else 128
            if jj >= 2:
                e.wait_ge(ge.h, ge_b + jj - 1)
            sh, _t = w_tgt[jj]
            e.dma_start(WBF[b].rearrange("p (c w) -> p c w", w=128)
                        [:, :, 0:ncols],
                        winpr[:, :, col0:col0 + ncols]).then_inc(sh, 16)
        for jj in (30, 31):
            e.wait_ge(*st_x[jj])

    @P.on("gpsimd")
    def _(e):
        for jj in jobs:
            b = jj % 2
            e.wait_ge(*w_tgt[jj])
            if jj >= 2:
                e.wait_ge(pe.h, pe_mm[(jj - 1, 0)])
            e.tensor_copy(WB[b], WBF[b]).then_inc(ge.h, 1)

    @P.on("tensor")
    def _(e):
        identb = cst["ident"]
        for jj in jobs:
            b = jj % 2
            ncols = 64 if jj == 34 else 128
            e.wait_ge(ge.h, ge_b + jj + 1)
            for tc in range(4):
                g4 = jj * 4 + tc
                bank = ps[g4 % 2]
                prev = g4 - 2
                if prev >= 0:
                    e.wait_ge(ve.h, ve_ev[(prev // 4, prev % 4)])
                for c in range(NCHUNK):
                    mm = e.matmul(bank.ap()[0:ncols, 0:512],
                                  WB[b][:, c * 128:c * 128 + ncols],
                                  XF[:, c * T + tc * 512:c * T + tc * 512 + 512],
                                  start=(c == 0), stop=(c == NCHUNK - 1))
                mm.then_inc(pe.h, 1)
            if jj < 32:
                e.wait_ge(se.h, se_cv[jj])
                for tb in range(16):
                    tbank = ps[4 + tb % 2].ap().bitcast(BF16)[0:128, 0:128]
                    pv = (jj, tb - 2) if tb >= 2 else ((jj - 1, 14 + tb) if jj >= 1 else None)
                    if pv is not None:
                        e.wait_ge(ve.h, ve_tre[pv])
                    e.transpose(tbank, CONV[:, tb * 128:(tb + 1) * 128],
                                identb).then_inc(pe.h, 1)

    @P.on("scalar")
    def _(e):
        for jj in jobs:
            if jj >= 34:
                continue
            e.wait_ge(ve.h, ve_cv[jj])
            if jj >= 1 and jj - 1 < 32:
                e.wait_ge(pe.h, pe_tr[(jj - 1, 15)])
            elif jj >= 1:
                e.wait_ge(ve.h, ve_bc.get(jj - 1, ve_cv[jj]))
            e.activation(CONV, CONVF, AF.Silu,
                         bias=CBPC[:, jj:jj + 1]).then_inc(se.h, 1)
            if jj >= 1 and (jj - 1) < 32:
                pj = jj - 1
                e.wait_ge(ve.h, ve_tre[(pj, 15)])
                sh, _t = st_x[pj]
                e.dma_start(scr_xc_ap.rearrange("(tb p) o -> p tb o", p=128)
                            [:, :, pj * 128:(pj + 1) * 128],
                            XCST[pj % 2].rearrange("p (tb o) -> p tb o", tb=16)
                            ).then_inc(sh, 16)

    @P.on("vector")
    def _(e):
        e.memset(XBCR[:, 0:4], 0.0).then_inc(ve.h, 1)
        for jj in jobs:
            b = jj % 2
            ncols = 64 if jj == 34 else 128
            for tc in range(4):
                e.wait_ge(pe.h, pe_mm[(jj, tc)])
                if jj == 0 and tc == 0:
                    e.wait_ge(ve.h, ve_pad)
                if jj >= 34:
                    e.tensor_copy(DTS[0:64, tc * 512:(tc + 1) * 512],
                                  ps[(jj * 4 + tc) % 2].ap()[0:64, 0:512]
                                  ).then_inc(ve.h, 1)
                else:
                    if tc == 0 and jj >= 1:
                        pass
                    e.tensor_copy(XBCR[:, 4 + tc * 512:4 + tc * 512 + 512],
                                  ps[(jj * 4 + tc) % 2].ap()).then_inc(ve.h, 1)
                e.wait_ge(ve.h, ve_ev[(jj, tc)])
            if jj < 34:
                e.wait_ge(se.h, se_cv[jj - 1] if jj >= 1 else 0)
                e.tensor_scalar(CONVF, XBCR[:, 4:4 + T],
                                CWPC[:, jj * 4 + 3:jj * 4 + 4], None,
                                AX.mult).then_inc(ve.h, 1)
                e.wait_ge(ve.h, ve_cv[jj] - 3)
                for kk in (2, 1, 0):
                    e.scalar_tensor_tensor(CONVF, XBCR[:, 1 + kk:1 + kk + T],
                                           CWPC[:, jj * 4 + kk:jj * 4 + kk + 1],
                                           CONVF, AX.mult,
                                           AX.add).then_inc(ve.h, 1)
                    e.wait_ge(ve.h, ve_cv[jj] - kk)
            if jj < 32:
                for tb in range(16):
                    e.wait_ge(pe.h, pe_tr[(jj, tb)])
                    if jj >= 2:
                        e.wait_ge(*st_x[jj - 2])
                    e.tensor_copy(XCST[jj % 2][:, tb * 128:(tb + 1) * 128],
                                  ps[4 + tb % 2].ap().bitcast(BF16)[0:128, 0:128]
                                  ).then_inc(ve.h, 1)
            elif jj < 34:
                e.wait_ge(se.h, se_cv[jj])
                dst = BT if jj == 32 else CT
                e.tensor_copy(dst, CONV).then_inc(ve.h, 1)
    ar.pop()


def _emit_mamba_dt(P: Prog, ar: Arena, cst, ps, DTS, LAM, SM):
    nc = P.nc
    ar.push()
    xd, pe, ve, se = P.XD, P.PE, P.VE, P.SE
    GB = ar.take(64, T, F32)
    ONES64 = ar.take(64, T, F32)

    ve0 = ve.n

    @P.on("vector")
    def _(e):
        k = ve0

        def step(ins):
            nonlocal k
            ins.then_inc(ve.h, 1)
            k += 1
            e.wait_ge(ve.h, k)
        step(e.memset(ONES64, 1.0))
        step(e.tensor_scalar(DTS, DTS, SM[0:64, 0:1], None, AX.add))
    ve_z = ve.inc(2)

    se0 = se.n

    @P.on("scalar")
    def _(e):
        e.wait_ge(ve.h, ve_z)
        e.activation(GB, DTS, AF.Abs).then_inc(se.h, 1)
        e.wait_ge(se.h, se0 + 1)
        e.activation(GB, GB, AF.Exp, scale=-1.0).then_inc(se.h, 1)
        e.wait_ge(se.h, se0 + 2)
        e.activation(GB, GB, AF.Ln, bias=cst["ones_pc"][0:64, 0:1]).then_inc(se.h, 1)
    se_sp = se.inc(3)

    ve1 = ve.n

    @P.on("vector")
    def _(e):
        k = ve1

        def step(ins):
            nonlocal k
            ins.then_inc(ve.h, 1)
            k += 1
            e.wait_ge(ve.h, k)
        e.wait_ge(se.h, se_sp)
        step(e.tensor_scalar(DTS, DTS, 0.0, None, AX.max))
        step(e.scalar_tensor_tensor(DTS, DTS, 1.0, GB, AX.mult, AX.add))
        step(e.tensor_scalar(GB, DTS, SM[0:64, 2:3], None, AX.mult))
        step(e.tensor_tensor_scan(LAM, ONES64, GB, 0.0, AX.mult, AX.add))
        for c in range(15, 0, -1):
            step(e.tensor_scalar(LAM[:, c * 128:(c + 1) * 128],
                                 LAM[:, c * 128:(c + 1) * 128],
                                 LAM[:, c * 128 - 1:c * 128], None, AX.subtract))
    ve.inc(4 + 15)
    ar.pop()


def _emit_mamba_m2(P: Prog, ar: Arena, cst, ps, BT, CT, DTS, LAM, NWB, DBC,
                   scr_xc_ap, scr_z_ap, scr_yn_ap):
    nc = P.nc
    ar.push()
    xd, pe, ve, se = P.XD, P.PE, P.VE, P.SE
    S = ar.take(128, 4096, F32)
    SB16 = ar.take(128, 4096, BF16)
    Y = ar.take(128, 4096, F32)
    XDE0 = ar.take(128, 4096, BF16)
    XDS = ar.take(128, 4096, BF16)
    XCD = ar.take(128, 4096, BF16)
    XC = [ar.take(128, 4096, BF16) for _ in range(2)]
    ZC = [ar.take(128, 4096, BF16) for _ in range(2)]
    GT = ar.take(128, 128, BF16)
    BTOK = ar.take(128, 128, BF16)
    ELT = ar.take(128, 64, F32)
    DTST = ar.take(128, 64, F32)
    EXPT = ar.take(128, 64, F32)
    EENDB = ar.take(128, 64, F32)
    EDL = ar.take(128, 64, F32)
    W3 = ar.take(128, 64, F32)
    LEROW = ar.take(1, 64, F32)
    LROW = ar.take(1, 16 * 128, F32)
    DSR0 = ar.take(128, 512, F32)
    DSR = [DSR0, DSR0]
    DSL = ar.take(128, 64, F32)
    AH = ar.take(128, 16 * 512, BF16)
    RS = ar.take(128, 4, F32)
    ZS = ar.take(128, 4096, BF16)
    YNT = ar.take(128, 4096, BF16)
    YNF0 = ar.take(128, 32 * 128, BF16)
    YNF = [YNF0, YNF0]

    xc_tgt = [P.xs.fill(c, 1) for c in range(16)]
    z_tgt = [P.ws2.fill(c, 1) for c in range(16)]
    st_y = [P.ss.fill(c, 1) for c in range(16)]
    lrow_tgt = [P.ws.fill(i % 16, 1) for i in range(64)]

    pe_m = {}
    se_m = {}
    ve_m = {}
    vv = ve.n + 1
    m2_pad = vv
    for c in range(16):
        pe_m[(c, "trl")] = pe.inc(1)
        pe_m[(c, "trd")] = pe.inc(1)
        pe_m[(c, "g")] = pe.inc(1)
        pe_m[(c, "trb")] = pe.inc(1)
        pe_m[(c, "tre")] = pe.inc(1)
        pe_m[(c, "bce")] = pe.inc(1)
        for gr in range(16):
            pe_m[(c, "bcg", gr)] = pe.inc(1)
        for hg in range(8):
            pe_m[(c, "in", hg)] = pe.inc(1)
        for hg in range(8):
            pe_m[(c, "ia", hg)] = pe.inc(1)
            pe_m[(c, "ds", hg)] = pe.inc(1)
        for tb in range(32):
            pe_m[(c, "yt", tb)] = pe.inc(1)
    for c in range(16):
        se_m[(c, "expt")] = se.inc(1)
        for gr in range(16):
            se_m[(c, "eds", gr)] = se.inc(1)
        se_m[(c, "eend")] = se.inc(1)
        se_m[(c, "edl")] = se.inc(1)
        se_m[(c, "zs")] = se.inc(1)
        se_m[(c, "sq")] = se.inc(1)
        se_m[(c, "rsq")] = se.inc(1)
    for c in range(16):
        vv += 1; ve_m[(c, "sb16")] = vv
        vv += 1; ve_m[(c, "elt")] = vv
        vv += 1; ve_m[(c, "dtst")] = vv
        vv += 1; ve_m[(c, "gt")] = vv
        vv += 1; ve_m[(c, "btok")] = vv
        vv += 1; ve_m[(c, "lerow")] = vv
        vv += 1; ve_m[(c, "xde0")] = vv
        vv += 1; ve_m[(c, "xcd")] = vv
        for gr in range(16):
            vv += 1; ve_m[(c, "dsub", gr)] = vv
            vv += 1; ve_m[(c, "dmin", gr)] = vv
            vv += 1; ve_m[(c, "ah", gr)] = vv
        vv += 1; ve_m[(c, "dsl")] = vv
        vv += 1; ve_m[(c, "w3")] = vv
        vv += 1; ve_m[(c, "xds")] = vv
        for hg in range(8):
            vv += 1; ve_m[(c, "t1", hg)] = vv
            vv += 1; ve_m[(c, "y", hg)] = vv
        for hg in range(8):
            vv += 1; ve_m[(c, "yi", hg)] = vv
            vv += 1; ve_m[(c, "sup1", hg)] = vv
            vv += 1; ve_m[(c, "sup2", hg)] = vv
        vv += 1; ve_m[(c, "yg")] = vv
        vv += 1; ve_m[(c, "r1")] = vv
        vv += 1; ve_m[(c, "r2")] = vv
        vv += 1; ve_m[(c, "yn")] = vv
        for tb in range(32):
            vv += 1; ve_m[(c, "ytev", tb)] = vv
    ve.n = vv

    @P.on("sync")
    def _(e):
        for c in range(16):
            b = c % 2
            if c >= 2:
                e.wait_ge(ve.h, ve_m[(c - 2, "xds")])
                e.wait_ge(se.h, se_m[(c - 2, "zs")])
            xh, _x = xc_tgt[c]
            e.dma_start(XC[b], scr_xc_ap[c * 128:(c + 1) * 128, :]).then_inc(xh, 16)
            zh, _z = z_tgt[c]
            e.dma_start(ZC[b], scr_z_ap[c * 128:(c + 1) * 128, :]).then_inc(zh, 16)
            for q in range(4):
                if q >= 1:
                    e.wait_ge(pe.h, pe_m[(c, "bcg", 4 * q - 1)])
                elif c >= 1:
                    e.wait_ge(pe.h, pe_m[(c - 1, "bcg", 15)])
                lh, _l = lrow_tgt[4 * c + q]
                e.dma_start(LROW, LAM[16 * q:16 * q + 16,
                                      c * 128:(c + 1) * 128]).then_inc(lh, 16)
        e.wait_ge(*st_y[15])
        e.wait_ge(*st_y[14])

    @P.on("tensor")
    def _(e):
        identf = cst["ident_f32"]
        identb = cst["ident"]
        onesf = cst["ones_row_f32"]
        for c in range(16):
            b = c % 2
            if c >= 1:
                e.wait_ge(ve.h, ve_m[(c - 1, "elt")])
            e.transpose(ps[6].ap()[0:128, 0:64],
                        LAM[:, c * 128:(c + 1) * 128],
                        identf[0:64, 0:64]).then_inc(pe.h, 1)
            if c >= 1:
                e.wait_ge(ve.h, ve_m[(c - 1, "dtst")])
                e.wait_ge(ve.h, ve_m[(c - 1, "ytev", 30)])
            e.transpose(ps[4].ap()[0:128, 0:64],
                        DTS[:, c * 128:(c + 1) * 128],
                        identf[0:64, 0:64]).then_inc(pe.h, 1)
            if c >= 1:
                e.wait_ge(ve.h, ve_m[(c - 1, "gt")])
                e.wait_ge(ve.h, ve_m[(c - 1, "ytev", 31)])
            e.matmul(ps[5].ap()[:, 0:128], BT[:, c * 128:(c + 1) * 128],
                     CT[:, c * 128:(c + 1) * 128],
                     start=True, stop=True).then_inc(pe.h, 1)
            if c >= 1:
                e.wait_ge(ve.h, ve_m[(c - 1, "btok")])
                e.wait_ge(se.h, se_m[(c - 1, "eend")])
            e.transpose(ps[7].ap().bitcast(BF16)[0:128, 0:128],
                        BT[:, c * 128:(c + 1) * 128],
                        identb).then_inc(pe.h, 1)
            e.wait_ge(ve.h, ve_m[(c, "elt")])
            e.transpose(ps[6].ap()[0:1, 64:128],
                        LAM[:, c * 128 + 127:c * 128 + 128],
                        identf[0:64, 0:64]).then_inc(pe.h, 1)
            e.wait_ge(ve.h, ve_m[(c, "lerow")])
            e.wait_ge(ve.h, ve_m[(c, "btok")])
            e.matmul(ps[7].ap()[:, 64:128], onesf, LEROW,
                     start=True, stop=True).then_inc(pe.h, 1)
            for gr in range(16):
                bgb = ps[gr % 2]
                if gr % 4 == 0:
                    e.wait_ge(*lrow_tgt[4 * c + gr // 4])
                if gr >= 2:
                    e.wait_ge(ve.h, ve_m[(c, "dsub", gr - 2)])
                elif c >= 1:
                    e.wait_ge(ve.h, ve_m[(c - 1, "t1", 6 + gr)])
                e.matmul(bgb.ap(), onesf,
                         LROW[0:1, (gr % 4) * 512:(gr % 4 + 1) * 512],
                         start=True, stop=True).then_inc(pe.h, 1)
            for hg in range(8):
                bank = ps[hg % 2]
                if hg >= 2:
                    e.wait_ge(ve.h, ve_m[(c, "t1", hg - 2)])
                else:
                    e.wait_ge(ve.h, ve_m[(c, "dsub", 14 + hg)])
                e.wait_ge(ve.h, ve_m[(c, "sb16")])
                e.matmul(bank.ap(), CT[:, c * 128:(c + 1) * 128],
                         SB16[:, hg * 512:(hg + 1) * 512],
                         start=True, stop=True).then_inc(pe.h, 1)
            for hg in range(8):
                bank = ps[hg % 2]
                bankc = ps[2 + hg % 2]
                e.wait_ge(ve.h, ve_m[(c, "t1", 6 + hg % 2)])
                if hg >= 2:
                    e.wait_ge(ve.h, ve_m[(c, "yi", hg - 2)])
                e.wait_ge(ve.h, ve_m[(c, "xde0")])
                for i8 in range(8):
                    h = hg * 8 + i8
                    e.wait_ge(ve.h, ve_m[(c, "ah", h // 4)])
                    mm = e.matmul(bank.ap()[:, i8 * 64:(i8 + 1) * 64],
                                  AH[:, h * 128:(h + 1) * 128],
                                  XDE0[:, h * 64:(h + 1) * 64],
                                  start=(i8 == 0), stop=(i8 == 7))
                mm.then_inc(pe.h, 1)
                if hg < 2:
                    if c >= 1:
                        e.wait_ge(ve.h, ve_m[(c - 1, "sup2", 6 + hg)])
                else:
                    e.wait_ge(ve.h, ve_m[(c, "sup2", hg - 2)])
                e.wait_ge(ve.h, ve_m[(c, "xds")])
                e.wait_ge(ve.h, ve_m[(c, "btok")])
                e.matmul(bankc.ap(), BTOK,
                         XDS[:, hg * 512:(hg + 1) * 512],
                         start=True, stop=True).then_inc(pe.h, 1)
            e.wait_ge(ve.h, ve_m[(c, "yn")])
            for tb in range(32):
                tbank = ps[4 + tb % 2].ap().bitcast(BF16)[0:128, 0:128]
                pv = (c, tb - 2) if tb >= 2 else ((c - 1, 30 + tb) if c >= 1 else None)
                if pv is not None:
                    e.wait_ge(ve.h, ve_m[(pv[0], "ytev", pv[1])])
                e.transpose(tbank, YNT[:, tb * 128:(tb + 1) * 128],
                            identb).then_inc(pe.h, 1)

    @P.on("scalar")
    def _(e):
        for c in range(16):
            b = c % 2
            e.wait_ge(ve.h, ve_m[(c, "elt")])
            if c >= 1:
                e.wait_ge(ve.h, ve_m[(c - 1, "t1", 7)])
            e.activation(EXPT, ELT, AF.Exp).then_inc(se.h, 1)
            for gr in range(16):
                e.wait_ge(ve.h, ve_m[(c, "dmin", gr)])
                e.activation(DSR[gr % 2], DSR[gr % 2], AF.Exp).then_inc(se.h, 1)
            e.wait_ge(pe.h, pe_m[(c, "bce")])
            if c >= 1:
                e.wait_ge(ve.h, ve_m[(c - 1, "sup1", 7)])
            e.activation(EENDB, ps[7].ap()[:, 64:128], AF.Exp).then_inc(se.h, 1)
            e.wait_ge(ve.h, ve_m[(c, "dsl")])
            if c >= 1:
                e.wait_ge(ve.h, ve_m[(c - 1, "w3")])
            e.activation(EDL, DSL, AF.Exp).then_inc(se.h, 1)
            e.wait_ge(*z_tgt[c])
            if c >= 1:
                e.wait_ge(ve.h, ve_m[(c - 1, "yg")])
                e.wait_ge(se.h, se_m[(c - 1, "sq")])
            e.activation(ZS, ZC[b], AF.Silu).then_inc(se.h, 1)
            e.wait_ge(ve.h, ve_m[(c, "yg")])
            e.activation(ZS, Y, AF.Square, accum_out=RS[:, 0:1]).then_inc(se.h, 1)
            e.wait_ge(ve.h, ve_m[(c, "r1")])
            e.activation(RS[:, 2:3], RS[:, 1:2], AF.Sqrt,
                         bias=cst["eps_pc"]).then_inc(se.h, 1)
            if c >= 1:
                pj = c - 1
                e.wait_ge(ve.h, ve_m[(pj, "ytev", 31)])
                sh, _t = st_y[pj]
                e.dma_start(scr_yn_ap.rearrange("(tb p) o -> p tb o", p=128)
                            [:, :, pj * 128:(pj + 1) * 128],
                            YNF[pj % 2].rearrange("p (tb o) -> p tb o", tb=32)
                            ).then_inc(sh, 16)
        pj = 15
        e.wait_ge(ve.h, ve_m[(pj, "ytev", 31)])
        sh, _t = st_y[pj]
        e.dma_start(scr_yn_ap.rearrange("(tb p) o -> p tb o", p=128)
                    [:, :, pj * 128:(pj + 1) * 128],
                    YNF[pj % 2].rearrange("p (tb o) -> p tb o", tb=32)
                    ).then_inc(sh, 16)

    @P.on("vector")
    def _(e):
        e.memset(S, 0.0).then_inc(ve.h, 1)
        for c in range(16):
            b = c % 2
            if c >= 1:
                e.wait_ge(ve.h, ve_m[(c - 1, "sup2", 7)])
            else:
                e.wait_ge(ve.h, m2_pad)
            e.tensor_copy(SB16, S).then_inc(ve.h, 1)
            e.wait_ge(pe.h, pe_m[(c, "trl")])
            if c >= 1:
                e.wait_ge(se.h, se_m[(c - 1, "expt")])
                e.wait_ge(ve.h, ve_m[(c - 1, "dsub", 15)])
            e.tensor_copy(ELT, ps[6].ap()[0:128, 0:64]).then_inc(ve.h, 1)
            e.wait_ge(pe.h, pe_m[(c, "trd")])
            if c >= 1:
                e.wait_ge(ve.h, ve_m[(c - 1, "xde0")])
                e.wait_ge(ve.h, ve_m[(c - 1, "w3")])
            e.tensor_copy(DTST, ps[4].ap()[0:128, 0:64]).then_inc(ve.h, 1)
            e.wait_ge(pe.h, pe_m[(c, "g")])
            e.scalar_tensor_tensor(GT, ps[5].ap()[:, 0:128], 1.0,
                                   cst["mask_ult"], AX.mult,
                                   AX.mult).then_inc(ve.h, 1)
            e.wait_ge(pe.h, pe_m[(c, "trb")])
            e.tensor_copy(BTOK, ps[7].ap().bitcast(BF16)[0:128, 0:128]
                          ).then_inc(ve.h, 1)
            e.wait_ge(pe.h, pe_m[(c, "tre")])
            if c >= 1:
                e.wait_ge(pe.h, pe_m[(c - 1, "bce")])
            e.tensor_copy(LEROW, ps[6].ap()[0:1, 64:128]).then_inc(ve.h, 1)
            e.wait_ge(*xc_tgt[c])
            e.wait_ge(ve.h, ve_m[(c, "dtst")])
            if c >= 1:
                e.wait_ge(pe.h, pe_m[(c - 1, "ia", 7)])
            e.tensor_tensor(XDE0.rearrange("p (h q) -> p h q", h=64),
                            XC[b].rearrange("p (h q) -> p h q", h=64),
                            DTST.unsqueeze(2).broadcast_to([128, 64, 64]),
                            AX.mult).then_inc(ve.h, 1)
            if c >= 1:
                e.wait_ge(ve.h, ve_m[(c - 1, "yi", 7)])
            e.tensor_tensor(XCD.rearrange("p (h q) -> p h q", h=64),
                            XC[b].rearrange("p (h q) -> p h q", h=64),
                            DBC.unsqueeze(2).broadcast_to([128, 64, 64]),
                            AX.mult).then_inc(ve.h, 1)
            for gr in range(16):
                bgb = ps[gr % 2]
                e.wait_ge(pe.h, pe_m[(c, "bcg", gr)])
                if gr >= 1:
                    e.wait_ge(se.h, se_m[(c, "eds", gr - 1)])
                    e.wait_ge(ve.h, ve_m[(c, "ah", gr - 1)])
                elif c >= 1:
                    e.wait_ge(se.h, se_m[(c - 1, "eds", 15)])
                    e.wait_ge(ve.h, ve_m[(c - 1, "ah", 15)])
                e.scalar_tensor_tensor(
                    DSR[gr % 2].rearrange("p (a t) -> p a t", a=4),
                    ELT[:, gr * 4:(gr + 1) * 4].unsqueeze(2)
                    .broadcast_to([128, 4, 128]),
                    -1.0, bgb.ap().rearrange("p (a t) -> p a t", a=4),
                    AX.mult, AX.add).then_inc(ve.h, 1)
                e.wait_ge(ve.h, ve_m[(c, "dsub", gr)])
                e.tensor_scalar(DSR[gr % 2], DSR[gr % 2], 0.0, None,
                                AX.min).then_inc(ve.h, 1)
                e.wait_ge(se.h, se_m[(c, "eds", gr)])
                if c >= 1 and gr < 2:
                    e.wait_ge(pe.h, pe_m[(c - 1, "ia", 7)])
                elif gr >= 2:
                    pass
                e.tensor_tensor(
                    AH[:, gr * 512:(gr + 1) * 512]
                    .rearrange("p (a t) -> p a t", a=4),
                    DSR[gr % 2].rearrange("p (a t) -> p a t", a=4),
                    GT.unsqueeze(1).broadcast_to([128, 4, 128]),
                    AX.mult).then_inc(ve.h, 1)
            e.wait_ge(pe.h, pe_m[(c, "bce")])
            e.wait_ge(se.h, se_m[(c, "eend")])
            if c >= 1:
                e.wait_ge(se.h, se_m[(c - 1, "edl")])
            e.scalar_tensor_tensor(DSL, ELT, -1.0, ps[7].ap()[:, 64:128],
                                   AX.mult, AX.add).then_inc(ve.h, 1)
            e.wait_ge(se.h, se_m[(c, "edl")])
            if c >= 1:
                e.wait_ge(ve.h, ve_m[(c - 1, "xds")])
            e.tensor_tensor(W3, DTST, EDL, AX.mult).then_inc(ve.h, 1)
            e.wait_ge(ve.h, ve_m[(c, "w3")])
            if c >= 1:
                e.wait_ge(pe.h, pe_m[(c - 1, "ds", 7)])
            e.tensor_tensor(XDS.rearrange("p (h q) -> p h q", h=64),
                            XC[b].rearrange("p (h q) -> p h q", h=64),
                            W3.unsqueeze(2).broadcast_to([128, 64, 64]),
                            AX.mult).then_inc(ve.h, 1)
            for hg in range(8):
                bank = ps[hg % 2]
                e.wait_ge(pe.h, pe_m[(c, "in", hg)])
                e.wait_ge(se.h, se_m[(c, "expt")])
                if hg == 0 and c >= 1:
                    e.wait_ge(ve.h, ve_m[(c - 1, "yn")])
                    e.wait_ge(se.h, se_m[(c - 1, "sq")])
                e.tensor_tensor(
                    Y.rearrange("p (h q) -> p h q", h=64)
                    [:, hg * 8:(hg + 1) * 8, :],
                    bank.ap().rearrange("p (h q) -> p h q", h=8),
                    EXPT[:, hg * 8:(hg + 1) * 8].unsqueeze(2)
                    .broadcast_to([128, 8, 64]), AX.mult).then_inc(ve.h, 1)
                sl = Y[:, hg * 512:(hg + 1) * 512]
                e.wait_ge(ve.h, ve_m[(c, "t1", hg)])
                e.tensor_tensor(sl, sl, XCD[:, hg * 512:(hg + 1) * 512],
                                AX.add).then_inc(ve.h, 1)
            for hg in range(8):
                bank = ps[hg % 2]
                bankc = ps[2 + hg % 2]
                sl = Y[:, hg * 512:(hg + 1) * 512]
                e.wait_ge(pe.h, pe_m[(c, "ia", hg)])
                e.tensor_tensor(sl, sl, bank.ap(), AX.add).then_inc(ve.h, 1)
                e.wait_ge(ve.h, ve_m[(c, "yi", hg)])
                e.wait_ge(se.h, se_m[(c, "eend")])
                ssl = S[:, hg * 512:(hg + 1) * 512]
                e.tensor_tensor(ssl.rearrange("p (h q) -> p h q", h=8),
                                ssl.rearrange("p (h q) -> p h q", h=8),
                                EENDB[:, hg * 8:(hg + 1) * 8].unsqueeze(2)
                                .broadcast_to([128, 8, 64]),
                                AX.mult).then_inc(ve.h, 1)
                e.wait_ge(ve.h, ve_m[(c, "sup1", hg)])
                e.wait_ge(pe.h, pe_m[(c, "ds", hg)])
                e.tensor_tensor(ssl, ssl, bankc.ap(), AX.add).then_inc(ve.h, 1)
            e.wait_ge(ve.h, ve_m[(c, "yi", 7)])
            e.wait_ge(se.h, se_m[(c, "zs")])
            e.tensor_tensor(Y, Y, ZS, AX.mult).then_inc(ve.h, 1)
            e.wait_ge(se.h, se_m[(c, "sq")])
            e.tensor_scalar(RS[:, 1:2], RS[:, 0:1], 1.0 / 4096.0, None,
                            AX.mult).then_inc(ve.h, 1)
            e.wait_ge(se.h, se_m[(c, "rsq")])
            e.reciprocal(RS[:, 3:4], RS[:, 2:3]).then_inc(ve.h, 1)
            e.wait_ge(ve.h, ve_m[(c, "r2")])
            if c >= 1:
                e.wait_ge(pe.h, pe_m[(c - 1, "yt", 31)])
            e.scalar_tensor_tensor(YNT, Y, RS[:, 3:4], NWB, AX.mult,
                                   AX.mult).then_inc(ve.h, 1)
            for tb in range(32):
                e.wait_ge(pe.h, pe_m[(c, "yt", tb)])
                if tb == 0 and c >= 1:
                    e.wait_ge(*st_y[c - 1])
                e.tensor_copy(YNF[b][:, tb * 128:(tb + 1) * 128],
                              ps[4 + tb % 2].ap().bitcast(BF16)[0:128, 0:128]
                              ).then_inc(ve.h, 1)
    ar.pop()


def _emit_mamba_outproj(P: Prog, ar: Arena, cst, ps, woutT_ap, scr_yn_ap,
                        xin_ap, xout_ap):
    nc = P.nc
    ar.push()
    YNS = ar.take(128, 32 * T, BF16)
    xd = P.XD
    xd0 = xd.n
    xd.inc(16)

    @P.on("sync")
    def _(e):
        e.dma_start(YNS.rearrange("p (c t) -> p c t", c=32),
                    scr_yn_ap.rearrange("(c p) t -> p c t", p=128)).then_inc(xd.h, 16)
        e.wait_ge(xd.h, xd0 + 16)
    P.barrier()

    def srcfn(c, t0, tw):
        return YNS[:, c * T + t0:c * T + t0 + tw]
    emit_outproj_residual(P, ar, cst, ps, srcfn, woutT_ap, None,
                          xin_ap, xout_ap, nsrc=32)
    ar.pop()


def _mamba_weight_layout(inputs):
    cw = inputs["L0_attn_a_conv_w"].reshape(4352, 4)
    return {
        "ma_winT": np.ascontiguousarray(inputs["L0_attn_a_W_in"].T),
        "ma_woutT": np.ascontiguousarray(inputs["L0_attn_a_W_out"].T),
        "ma_cw": np.ascontiguousarray(
            cw.reshape(34, 128, 4).transpose(1, 0, 2).reshape(128, 136)),
        "ma_cb": np.ascontiguousarray(
            inputs["L0_attn_a_conv_b"].reshape(34, 128).T),
        "ma_dtb": np.ascontiguousarray(
            np.pad(inputs["L0_attn_a_dt_bias"], (0, 64)).reshape(128, 1)),
        "ma_alog": np.ascontiguousarray(
            np.pad(inputs["L0_attn_a_A_log"], (0, 64)).reshape(128, 1)),
        "ma_drow": np.ascontiguousarray(inputs["L0_attn_a_D"].reshape(1, 64)),
        "ma_nwrow": np.ascontiguousarray(
            inputs["L0_attn_a_norm_w"].reshape(1, 4096)),
    }


def build_mamba_test():
    nc = bass.Bass("TRN2", target_bir_lowering=False, debug=False)
    P = Prog(nc)
    xin = nc.dram_tensor("xin", [DM, T], F32, kind="ExternalInput").ap()
    xout = nc.dram_tensor("xout", [DM, T], F32, kind="ExternalOutput").ap()
    winT = nc.dram_tensor("ma_winT", [DM, 8512], F32, kind="ExternalInput").ap()
    woutT = nc.dram_tensor("ma_woutT", [4096, DM], F32, kind="ExternalInput").ap()
    cw = nc.dram_tensor("ma_cw", [128, 136], F32, kind="ExternalInput").ap()
    cb = nc.dram_tensor("ma_cb", [128, 34], F32, kind="ExternalInput").ap()
    dtb = nc.dram_tensor("ma_dtb", [128, 1], F32, kind="ExternalInput").ap()
    alog = nc.dram_tensor("ma_alog", [128, 1], F32, kind="ExternalInput").ap()
    drow = nc.dram_tensor("ma_drow", [1, 64], F32, kind="ExternalInput").ap()
    nwrow = nc.dram_tensor("ma_nwrow", [1, 4096], F32, kind="ExternalInput").ap()
    scr_xc = nc.dram_tensor("scr_xc", [T, 4096], BF16, kind="Internal").ap()
    scr_z = nc.dram_tensor("scr_z", [T, 4096], BF16, kind="Internal").ap()
    scr_yn = nc.dram_tensor("scr_yn", [4096, T], BF16, kind="Internal").ap()

    ps = [nc.alloc_psum_tensor(f"ps{i}", [128, 512], F32) for i in range(8)]
    ar = Arena(nc, "AR", 192 * 1024)
    cst = Consts(nc, P, host_consts())

    emit_mamba(P, ar, cst, ps, xin, xout, winT, woutT, cw, cb, dtb, alog,
               drow, nwrow, scr_xc, scr_z, scr_yn)
    P.finish()
    return nc


def _pow_weight_layout(inputs):
    return {
        "pw_wq": np.ascontiguousarray(inputs["L0_attn_b_Wq"]),
        "pw_wk": np.ascontiguousarray(inputs["L0_attn_b_Wk"]),
        "pw_wv": np.ascontiguousarray(inputs["L0_attn_b_Wv"]),
        "pw_wg": np.ascontiguousarray(inputs["L0_attn_b_Wg"]),
        "pw_wc": np.ascontiguousarray(inputs["L0_attn_b_Wc"]),
        "pw_bpc": np.ascontiguousarray(
            inputs["L0_attn_b_bc"].reshape(NCHUNK, 128).T),
    }


def build_pow_test():
    nc = bass.Bass("TRN2", target_bir_lowering=False, debug=False)
    P = Prog(nc)
    xin = nc.dram_tensor("xin", [DM, T], F32, kind="ExternalInput").ap()
    xout = nc.dram_tensor("xout", [DM, T], F32, kind="ExternalOutput").ap()
    wq = nc.dram_tensor("pw_wq", [DM, DM], F32, kind="ExternalInput").ap()
    wk = nc.dram_tensor("pw_wk", [DM, 1024], F32, kind="ExternalInput").ap()
    wv = nc.dram_tensor("pw_wv", [DM, 1024], F32, kind="ExternalInput").ap()
    wg = nc.dram_tensor("pw_wg", [DM, 8], F32, kind="ExternalInput").ap()
    wc = nc.dram_tensor("pw_wc", [DM, DM], F32, kind="ExternalInput").ap()
    bpc = nc.dram_tensor("pw_bpc", [128, NCHUNK], F32, kind="ExternalInput").ap()
    scrq = nc.dram_tensor("scrq", [DM, T], BF16, kind="Internal").ap()
    scrk = nc.dram_tensor("scrk", [1024, T], BF16, kind="Internal").ap()
    scrv = nc.dram_tensor("scrv", [T, 1024], BF16, kind="Internal").ap()
    scr_eq = nc.dram_tensor("scr_eq", [1024, T], F32, kind="Internal").ap()
    scr_ek = nc.dram_tensor("scr_ek", [1024, T], F32, kind="Internal").ap()

    ps = [nc.alloc_psum_tensor(f"ps{i}", [128, 512], F32) for i in range(8)]
    ar = Arena(nc, "AR", 192 * 1024)
    cst = Consts(nc, P, host_consts())

    emit_pow(P, ar, cst, ps, xin, xout, wq, wk, wv, wg, wc, bpc,
             scrq, scrk, scrv, scr_eq, scr_ek)
    P.finish()
    return nc


def _moe_weight_layout(inputs):
    aw = np.ascontiguousarray(inputs["L1_ffn_A_w"].reshape(64, DM).T)
    wr = np.ascontiguousarray(inputs["L1_ffn_W_router"].T)
    return {
        "wbaseT": np.ascontiguousarray(inputs["L1_ffn_W_base"].T),
        "aw72": np.ascontiguousarray(np.concatenate([aw, wr], axis=1)),
        "bflat": np.ascontiguousarray(
            inputs["L1_ffn_B_w"].transpose(0, 2, 1).reshape(64, DM)),
        "bpc": np.ascontiguousarray(
            inputs["L1_ffn_b_base"].reshape(NCHUNK, 128).T),
    }


def build_moe_test():
    nc = bass.Bass("TRN2", target_bir_lowering=False, debug=False)
    P = Prog(nc)
    xin = nc.dram_tensor("xin", [DM, T], F32, kind="ExternalInput").ap()
    xout = nc.dram_tensor("xout", [DM, T], F32, kind="ExternalOutput").ap()
    wbaseT = nc.dram_tensor("wbaseT", [DM, DM], F32, kind="ExternalInput").ap()
    aw72 = nc.dram_tensor("aw72", [DM, 72], F32, kind="ExternalInput").ap()
    bflat = nc.dram_tensor("bflat", [64, DM], F32, kind="ExternalInput").ap()
    bpc = nc.dram_tensor("bpc", [128, NCHUNK], F32, kind="ExternalInput").ap()
    WBA = nc.dram_tensor("WBA_bf", [DM, DM], BF16, kind="Internal").ap()

    ps = [nc.alloc_psum_tensor(f"ps{i}", [128, 512], F32) for i in range(8)]
    ar = Arena(nc, "AR", 192 * 1024)
    cst = Consts(nc, P, host_consts())

    emit_cast_to_bf16(P, ar, wbaseT, WBA, DM, DM)
    emit_moe(P, ar, cst, ps, xin, xout, WBA, aw72, bflat, bpc)
    P.finish()
    return nc


def _gqa_weight_layout(inputs):
    d = {
        "gq_wq": np.ascontiguousarray(inputs["L1_attn_b_Wq"]),
        "gq_wk": np.ascontiguousarray(inputs["L1_attn_b_Wk"]),
        "gq_wv": np.ascontiguousarray(inputs["L1_attn_b_Wv"]),
        "gq_wo": np.ascontiguousarray(inputs["L1_attn_b_Wo"]),
        "gq_bpc": np.ascontiguousarray(
            inputs["L1_attn_b_b_out"].reshape(NCHUNK, 128).T),
    }
    d.update(host_gqa_tables())
    return d


def build_gqa_test(stage=3):
    nc = bass.Bass("TRN2", target_bir_lowering=False, debug=False)
    P = Prog(nc)
    xin = nc.dram_tensor("xin", [DM, T], F32, kind="ExternalInput").ap()
    xout = nc.dram_tensor("xout", [DM, T], F32, kind="ExternalOutput").ap()
    wq = nc.dram_tensor("gq_wq", [DM, DM], F32, kind="ExternalInput").ap()
    wk = nc.dram_tensor("gq_wk", [DM, 512], F32, kind="ExternalInput").ap()
    wv = nc.dram_tensor("gq_wv", [DM, 512], F32, kind="ExternalInput").ap()
    wo = nc.dram_tensor("gq_wo", [DM, DM], F32, kind="ExternalInput").ap()
    bpc = nc.dram_tensor("gq_bpc", [128, NCHUNK], F32, kind="ExternalInput").ap()
    tbl_aps = {}
    for name in ("gq_cos_q", "gq_sin_q", "gq_cos_k", "gq_sin_k"):
        tbl_aps[name] = nc.dram_tensor(name, [128, T], F32, kind="ExternalInput").ap()
    scrq = nc.dram_tensor("scrq", [DM, T], BF16, kind="Internal").ap()
    scrk = nc.dram_tensor("scrk", [512, T], BF16, kind="Internal").ap()
    scrv = nc.dram_tensor("scrv", [T, 8 * 65], BF16, kind="Internal").ap()

    ps = [nc.alloc_psum_tensor(f"ps{i}", [128, 512], F32) for i in range(8)]
    ar = Arena(nc, "AR", 192 * 1024)
    cst = Consts(nc, P, host_consts())

    dbg = nc.dram_tensor("dbg", [DM, T], BF16, kind="Internal").ap()
    emit_gqa(P, ar, cst, ps, xin, xout, wq, wk, wv, wo, bpc,
             scrq, scrk, scrv, tbl_aps, dbg_ap=(dbg if stage >= 3 else None),
             stage=stage)
    P.finish()
    return nc


def build_full():
    nc = bass.Bass("TRN2", target_bir_lowering=False, debug=False)
    P = Prog(nc)
    xin = nc.dram_tensor("xin", [DM, T], F32, kind="ExternalInput").ap()
    xout = nc.dram_tensor("xout", [DM, T], F32, kind="ExternalOutput").ap()
    xm = [nc.dram_tensor(f"xm{i}", [DM, T], F32, kind="Internal").ap()
          for i in range(7)]

    ma_winT = nc.dram_tensor("ma_winT", [DM, 8512], F32, kind="ExternalInput").ap()
    ma_woutT = nc.dram_tensor("ma_woutT", [4096, DM], F32, kind="ExternalInput").ap()
    ma_cw = nc.dram_tensor("ma_cw", [128, 136], F32, kind="ExternalInput").ap()
    ma_cb = nc.dram_tensor("ma_cb", [128, 34], F32, kind="ExternalInput").ap()
    ma_dtb = nc.dram_tensor("ma_dtb", [128, 1], F32, kind="ExternalInput").ap()
    ma_alog = nc.dram_tensor("ma_alog", [128, 1], F32, kind="ExternalInput").ap()
    ma_drow = nc.dram_tensor("ma_drow", [1, 64], F32, kind="ExternalInput").ap()
    ma_nwrow = nc.dram_tensor("ma_nwrow", [1, 4096], F32, kind="ExternalInput").ap()
    ma_scr_xc = nc.dram_tensor("scr_xc", [T, 4096], BF16, kind="Internal").ap()
    ma_scr_z = nc.dram_tensor("scr_z", [T, 4096], BF16, kind="Internal").ap()
    ma_scr_yn = nc.dram_tensor("scr_yn", [4096, T], BF16, kind="Internal").ap()

    wqg = nc.dram_tensor("wqgT", [DM, HID], I32, kind="ExternalInput").ap()
    sg = nc.dram_tensor("s_gate", [HID], F32, kind="ExternalInput").ap()
    wqu = nc.dram_tensor("wquT", [DM, HID], I32, kind="ExternalInput").ap()
    su = nc.dram_tensor("s_up", [HID], F32, kind="ExternalInput").ap()
    wqd = nc.dram_tensor("wqdT", [HID, DM], I32, kind="ExternalInput").ap()
    sd = nc.dram_tensor("s_down", [DM], F32, kind="ExternalInput").ap()
    WG = nc.dram_tensor("WG_bf", [DM, HID], BF16, kind="Internal").ap()
    WU = nc.dram_tensor("WU_bf", [DM, HID], BF16, kind="Internal").ap()
    WDq = nc.dram_tensor("WD_bf", [HID, DM], BF16, kind="Internal").ap()

    pw_wq = nc.dram_tensor("pw_wq", [DM, DM], F32, kind="ExternalInput").ap()
    pw_wk = nc.dram_tensor("pw_wk", [DM, 1024], F32, kind="ExternalInput").ap()
    pw_wv = nc.dram_tensor("pw_wv", [DM, 1024], F32, kind="ExternalInput").ap()
    pw_wg = nc.dram_tensor("pw_wg", [DM, 8], F32, kind="ExternalInput").ap()
    pw_wc = nc.dram_tensor("pw_wc", [DM, DM], F32, kind="ExternalInput").ap()
    pw_bpc = nc.dram_tensor("pw_bpc", [128, NCHUNK], F32, kind="ExternalInput").ap()
    pw_scrq = nc.dram_tensor("pw_scrq", [DM, T], BF16, kind="Internal").ap()
    pw_scrk = nc.dram_tensor("pw_scrk", [1024, T], BF16, kind="Internal").ap()
    pw_scrv = nc.dram_tensor("pw_scrv", [T, 1024], BF16, kind="Internal").ap()
    pw_scr_eq = nc.dram_tensor("pw_scr_eq", [1024, T], F32, kind="Internal").ap()
    pw_scr_ek = nc.dram_tensor("pw_scr_ek", [1024, T], F32, kind="Internal").ap()

    sw_wq = nc.dram_tensor("sw_wq", [DM, DM], F32, kind="ExternalInput").ap()
    sw_wk = nc.dram_tensor("sw_wk", [DM, DM], F32, kind="ExternalInput").ap()
    sw_wv = nc.dram_tensor("sw_wv", [DM, DM], F32, kind="ExternalInput").ap()
    sw_wo = nc.dram_tensor("sw_wo", [DM, DM], F32, kind="ExternalInput").ap()
    sw_scrq = nc.dram_tensor("sw_scrq", [DM, T], BF16, kind="Internal").ap()
    sw_scrk = nc.dram_tensor("sw_scrk", [DM, T], BF16, kind="Internal").ap()
    sw_scrv = nc.dram_tensor("sw_scrv", [T, DM], BF16, kind="Internal").ap()

    wbaseT = nc.dram_tensor("wbaseT", [DM, DM], F32, kind="ExternalInput").ap()
    aw72 = nc.dram_tensor("aw72", [DM, 72], F32, kind="ExternalInput").ap()
    bflat = nc.dram_tensor("bflat", [64, DM], F32, kind="ExternalInput").ap()
    bpc = nc.dram_tensor("bpc", [128, NCHUNK], F32, kind="ExternalInput").ap()
    WBA = nc.dram_tensor("WBA_bf", [DM, DM], BF16, kind="Internal").ap()

    gq_wq = nc.dram_tensor("gq_wq", [DM, DM], F32, kind="ExternalInput").ap()
    gq_wk = nc.dram_tensor("gq_wk", [DM, 512], F32, kind="ExternalInput").ap()
    gq_wv = nc.dram_tensor("gq_wv", [DM, 512], F32, kind="ExternalInput").ap()
    gq_wo = nc.dram_tensor("gq_wo", [DM, DM], F32, kind="ExternalInput").ap()
    gq_bpc = nc.dram_tensor("gq_bpc", [128, NCHUNK], F32, kind="ExternalInput").ap()
    tbl_aps = {}
    for name in ("gq_cos_q", "gq_sin_q", "gq_cos_k", "gq_sin_k"):
        tbl_aps[name] = nc.dram_tensor(name, [128, T], F32,
                                       kind="ExternalInput").ap()
    gq_scrq = nc.dram_tensor("gq_scrq", [DM, T], BF16, kind="Internal").ap()
    gq_scrk = nc.dram_tensor("gq_scrk", [512, T], BF16, kind="Internal").ap()
    gq_scrv = nc.dram_tensor("gq_scrv", [T, 8 * 65], BF16, kind="Internal").ap()
    gq_dbg = nc.dram_tensor("gq_dbg", [DM, T], BF16, kind="Internal").ap()

    ps = [nc.alloc_psum_tensor(f"ps{i}", [128, 512], F32) for i in range(8)]
    ar = Arena(nc, "AR", 192 * 1024)
    cst = Consts(nc, P, host_consts())

    emit_dequant(P, ar, cst, ps, wqg, sg, WG, DM, HID)
    emit_dequant(P, ar, cst, ps, wqu, su, WU, DM, HID)
    emit_dequant(P, ar, cst, ps, wqd, sd, WDq, HID, DM)
    emit_cast_to_bf16(P, ar, wbaseT, WBA, DM, DM)
    P.barrier()
    emit_mamba(P, ar, cst, ps, xin, xm[0], ma_winT, ma_woutT, ma_cw, ma_cb,
               ma_dtb, ma_alog, ma_drow, ma_nwrow, ma_scr_xc, ma_scr_z,
               ma_scr_yn)
    P.barrier()
    emit_ffn(P, ar, cst, ps, xm[0], xm[1], WG, WU, WDq)
    P.barrier()
    emit_pow(P, ar, cst, ps, xm[1], xm[2], pw_wq, pw_wk, pw_wv, pw_wg, pw_wc,
             pw_bpc, pw_scrq, pw_scrk, pw_scrv, pw_scr_eq, pw_scr_ek)
    P.barrier()
    emit_ffn(P, ar, cst, ps, xm[2], xm[3], WG, WU, WDq)
    P.barrier()
    emit_swa(P, ar, cst, ps, xm[3], xm[4], sw_wq, sw_wk, sw_wv, sw_wo,
             sw_scrq, sw_scrk, sw_scrv)
    P.barrier()
    emit_moe(P, ar, cst, ps, xm[4], xm[5], WBA, aw72, bflat, bpc)
    P.barrier()
    emit_gqa(P, ar, cst, ps, xm[5], xm[6], gq_wq, gq_wk, gq_wv, gq_wo,
             gq_bpc, gq_scrq, gq_scrk, gq_scrv, tbl_aps, dbg_ap=gq_dbg,
             stage=3)
    P.barrier()
    emit_moe(P, ar, cst, ps, xm[6], xout, WBA, aw72, bflat, bpc)
    P.finish()
    return nc


def full_weight_map(inputs):
    w = {}
    w.update(_mamba_weight_layout(inputs))
    w.update(_ffn_weight_layout(inputs))
    w.update(_pow_weight_layout(inputs))
    w.update(_swa_weight_layout(inputs))
    w.update(_moe_weight_layout(inputs))
    w.update(_gqa_weight_layout(inputs))
    return w


FUSED = True


def kernel(**inputs):
    from concourse.bass_utils import run_bass_kernel_spmd
    if FUSED:
        x = np.asarray(inputs["L0_attn_a_u"], dtype=np.float32)
        nb = x.shape[0]
        consts = host_consts()
        w = full_weight_map(inputs)
        nc = build_full()
        ins = []
        for b in range(nb):
            m = dict(w)
            m["xin"] = np.ascontiguousarray(x[b].T)
            for name, arr in consts.items():
                m["cst_" + name] = arr
            ins.append(m)
        res = run_bass_kernel_spmd(nc, ins, core_ids=list(range(nb)))
        out = np.stack([np.asarray(res.results[b]["xout"]).T
                        for b in range(nb)])
        return np.ascontiguousarray(out.astype(np.float32))
    return _kernel_unfused(**inputs)


def _kernel_unfused(**inputs):
    from concourse.bass_utils import run_bass_kernel_spmd
    x = np.asarray(inputs["L0_attn_a_u"], dtype=np.float32)
    nb = x.shape[0]
    consts = host_consts()

    def run(build_fn, w, xcur):
        nc = build_fn()
        ins = []
        for b in range(nb):
            m = dict(w)
            m["xin"] = np.ascontiguousarray(xcur[b].T)
            for name, arr in consts.items():
                m["cst_" + name] = arr
            ins.append(m)
        res = run_bass_kernel_spmd(nc, ins, core_ids=list(range(nb)))
        return np.stack([np.asarray(res.results[b]["xout"]).T
                         for b in range(nb)])

    w_ffn = _ffn_weight_layout(inputs)
    w_moe = _moe_weight_layout(inputs)
    x = run(build_mamba_test, _mamba_weight_layout(inputs), x)
    x = run(build_ffn_test, w_ffn, x)
    x = run(build_pow_test, _pow_weight_layout(inputs), x)
    x = run(build_ffn_test, w_ffn, x)
    x = run(build_swa_test, _swa_weight_layout(inputs), x)
    x = run(build_moe_test, w_moe, x)
    x = run(build_gqa_test, _gqa_weight_layout(inputs), x)
    x = run(build_moe_test, w_moe, x)
    return np.ascontiguousarray(x.astype(np.float32))


if __name__ == "__main__":
    nc = build_ffn_test()
    print("built, instructions:", nc.n_instructions())
